# version 06 (final)
# baseline (speedup 1.0000x reference)
import numpy as np
import ml_dtypes
from contextlib import ExitStack

import concourse.bass as bass
import concourse.mybir as mybir
from concourse.bass_utils import run_bass_kernel_spmd

F32 = mybir.dt.float32
BF16 = mybir.dt.bfloat16
I32 = mybir.dt.int32
AF = mybir.ActivationFunctionType
ALU = mybir.AluOpType
AXX = mybir.AxisListType.X
IOff = bass.IndirectOffsetOnAxis

S = 4096
D = 1024
NT = S // 128
NDC = D // 128
H = 16
HD = 64
FF = 4096
NFC = FF // 128
NB = 4
NITER = 46


class Sem:
    def __init__(self, handle):
        self.h = handle
        self.n = 0

    def inc(self, inst, k=1):
        inst.then_inc(self.h, k)
        self.n += k
        return inst


def _host_constants():
    c = {}
    c["IDENT"] = np.eye(128, dtype=ml_dtypes.bfloat16)
    c["ONES"] = np.ones((128, 128), dtype=ml_dtypes.bfloat16)
    c["LTRI"] = (np.arange(128)[:, None] < np.arange(128)[None, :]).astype(ml_dtypes.bfloat16)
    c["IOTA_F"] = np.broadcast_to(np.arange(512, dtype=np.float32), (128, 512)).copy()
    c["IOTA_P_F"] = np.arange(128, dtype=np.float32).reshape(128, 1).copy()
    c["IOTA_P_B"] = np.arange(128, dtype=np.float32).reshape(128, 1).astype(ml_dtypes.bfloat16)
    c["CHUNKV_B"] = np.broadcast_to(np.arange(32, dtype=np.float32)[None, :], (128, 32)).astype(ml_dtypes.bfloat16).copy()
    _INV_BITS = [
        0x3f800000, 0x3f3ff911, 0x3f0ff59a, 0x3ed7e89b, 0x3ea1e89b, 0x3e72d423,
        0x3e361887, 0x3e088d77, 0x3dcccccd, 0x3d99940d, 0x3d6655c2, 0x3d2cba15,
        0x3d0186e3, 0x3cc2434f, 0x3c91ad39, 0x3c5a7bf2, 0x3c23d70a, 0x3bf5b9b0,
        0x3bb8449c, 0x3b8a2e77, 0x3b4f3e38, 0x3b1b690d, 0x3ae91528, 0x3aaec98e,
        0x3a83126f, 0x3a44948c, 0x3a136a16, 0x39dd1725, 0x39a5cb60, 0x3978a815,
        0x393a7753, 0x390bd472,
    ]
    inv = np.array(_INV_BITS, dtype=np.uint32).view(np.float32)
    ang = (np.arange(S, dtype=np.float32)[:, None] * inv[None, :]).astype(np.float32)
    a64 = ang.astype(np.float64)
    cosb = np.cos(a64).astype(np.float32)
    sinb = np.sin(a64).astype(np.float32)
    import base64 as _b64, zlib as _zlib
    for _tab, _blob in ((cosb, _COS_D), (sinb, _SIN_D)):
        _pk = np.frombuffer(_zlib.decompress(_b64.b85decode(_blob)), dtype=np.uint32)
        _bits = _tab.view(np.int32).ravel()
        _bits[(_pk >> 1)] += np.where((_pk & 1) == 1, 1, -1).astype(np.int32)
    cos = np.concatenate([cosb.astype(np.float64), cosb.astype(np.float64)], axis=1)
    sin = np.concatenate([-sinb.astype(np.float64), sinb.astype(np.float64)], axis=1)
    c["COS_T"] = np.tile(cos, (1, H)).astype(np.float32)
    c["SIN_TS"] = np.tile(sin, (1, H)).astype(np.float32)
    return c


_COS_D = "c-j@$cYKdm8^`hQIY*>M5Szpn8haMi7ESFZHDW74?7e4)y@?={+7YqUh!`FAN@E341fisaM#Kn8pHKdHzwY0?&biKYUFXg#xvX?Di`3pj%Ihh)Y|;vJ&MuummmE@JPUMl!<&}!(mzp6E-7&wAWC}|@2reSEE-IDrks=Y|D=ok+Bo$|j64H^9($!MZN2R44Wtba9%1TYq%a8XEi?7Q|ZLtR4m81y7V_s#ccU7rTH7TmPl#GHkxL#8#P)mA%mY+x$>PR!|at>EdI`b*9ekOJNTw48w)TaUKM0_LO`%(%DAoj1MV~9gMiZzxlV?z@u0rQ&j8)+@1t}Uf-wErLLM4v$7X(jo!CZ0A@fBfB6@(3a?*xXLKk0KqU6m07#g>@zd97X#s%#EvEr90gu|6t<oNsf9EGhB$Y3O)NsTlzA`KxqKN;6GSug`gqQ4GbJAEyPx680RQ8oO5`M;5QPH44;u4kz*9K87-Z{tuc&0R`MN39wth8CK2;@(vBab^SA`h$@IV!DgRXI%W2eSCePr^ES{fDo^Twu=SYR-(koaFuX$1>Y@aW^hyOxp8T><~#%PAN2*(!WStR8Sle!@U-ivt`&6h~omP!E#31@!fSk7;R;t9(9C{0_zJ;cI$CHcXwRpc6{FmE0A;kjPwfT|nlNu*|qU{7q6K1TmY=`0RzA`ekKw^@3M*sb*cHtK`^*sz^`+QBopx?38wM;e0Vn6_8CiI9EN2gl(NE!{;5{P#=iV(9Gy)Dn3Qa{Ul_J}eEzN}R-16#a!dqB4e{?5{kF)4y><&)@0)BT}1V(hT?=Cx?i{Hatb~6O4=4%;QP+Dyp5P4^ZU{`8rD-uo^?pal}rf#S&i}V_?Ju&M!)DkmV9_Ar$N4rQKJE@fxup5{DC{#n<V_KZyyEH^>Jn+>){;(i^wguSx8``&@fKP9I8z|7H(9;`d|rAbeA(3s(O_9#hFPyq_`FbLk%9(^)?zzhGZs!Arg?ugGNv`x%Yiat*oPkvk0gmwrG33ccqU3`fWZ>i8dX;kt~58KWlFs6-Z{U5G_f52FhxlGSJ(ig+3g$5R~7W^@V(D3gQh2u0<bM(w?fj-qBRqc*vXn&&ZU4Nq^Qg6M+VsGW}^N*6NPfM<n`(sAC`s8}(hKnyN!w7rDUm6Ap|OB)r&U|c9;w6LsEc|W7mNJsf{Mse^h&-?yHqbhI@*(wqfzN}<4r?Sy|WUFG78y%|}?MA!mM&)W4&BE@QMk{J@UfZZI=D@qI(ZfvjjD~zlEcJ~VVFczO*JnmMJ~w*)g;C1}JcIdIi!Kd~x-~M2!ftp55DQW<tcg+Yrbbh-2v3_C?P<=qEsU<CRiM!^^lZ&FMB;oK-fPQmbPD1+a<?Nl_^LhUn1W#)jLvo>7oCi%qf}?=g*5ndF}jCT1aze?-x%fX#&vAK5nRB%%xm3^@&p?V=)w9>pr?^Py5KCzgm4{~dK*RbHF|{g{g?;-{f&aK9%m3bfaeDiBZ>?%GJ}o6um#aWjA{&J{pc}_T;X}9;jCpO=SW1$(exGe;5OpM@ZMPRhhGsf-st57);-Z^Css@{I)?<@#2tivOU}PDT7m~~-}4T9exQHQ70dA_%1$OH$TG!f4F153sYXjtdK%A8r=~NR3w9R$f=76X;<L#C)}YEf_5rS-!vdqLXt0pIgz!*$a1nXO;xMDeOW8lm7!w(&vYcLCNo=c(enQ}C_U9V<3L|g@KI`Z?q^;-P2KF#6VnhV}wvir)Vvi$rvr(<B^uu;)xr5%qV{F<<J@E*hyT}<*(fwzmWqXX05WJUKB5WV=qg*s&;q-p)BO!*ph*Afb6SJ`w`;d0X$m<uQd?<;o7>Js`GCyYh#&bwR$={7m!s`h88o}6&<jnJ<JcHfGs6Uz=ryhtx=@aw@uAL+wr>VsmYK(a7KF7LZ$;lslJK~H|Q04;P38bU(Me>GGn1#Jaf!8H+ciCtGUckgN{}p1r&K`l!pY$Bu4I@81$UNROT8TI`OyvBwQ3&Et{tmy<2hHv>4os5K<a^{55AHJ$YW>AG1`&9Hvk#~Vl9BHrdlO#Ctod)|z$olXVXP<g!&B<>k5QdezP~tzTF>~7BPPu#dpfnpCdA_b9^y4TUr<-XAjd0yBll}^lR@4P{KjY;%v+<ZDDjTI!ff0@(7&u58{hLc0!KcuXOQzhql&0ztX^ZTwOR`=XXS%xn1hQ*$~@0vbrVUrk=4rYBdbB#td_zvyH!cd!lE2jDY%-`s-l<G54m{<U+1x^<83t#DS553=d+rEZTYQ^7qFUI&?<i+D_6woI))cz9v`b%UyjABUgB+Wt1nAfHOI4(R&k}QmX)^JT87xl5{sWzt8!N1A6wnW7=No%h=WfBt0O2_$*OcUtJZi4-|AM?(FjZ6Rg-gEtHnPnpID7R^Ey^T5REvbp<P`ok9t;(5%ekdkb)8Qt!jT})gQG#x0;6OUvRyF)lZ0SX!Rt(D)cMz0@v7TKVD&M6Rx3BQ>zXLL3-w~nbly#VQh1&1-Jm8mR3CjiQ{Y5ig@H|#k}Z+!L2!OW7QDdFc_`dT5X4CkX1G`g?~G%NSsEW_WVZe4pw_{v?JqpvKrOJY9V~PTHVH@Z}{EKYJPXC0m0-MNtoV)=g_LBRn1=13r~?h#44({)hB(J6P^27J;acH^c`~dXKwTzNRF^+uvPF7a)UE?Ig}WNa~+jOSan33k<@b(V;~kgMq3phW7P>BW2xylt|1yF#&g6a+`?02n@DYO6-6eI2h9AIwS7mt-%}&(N59GB9m&``#VT;B)#Pbb3CK6ysyrIaAdZ>TVHRWHt2yKv;d80Me5<kxs4=vVoS-o};Rt$%lE+0>dBPYAA1}7ryoA~#aw)%&Kb*eBdThY;WyJg=Il@>(q25Y*5z#+c9b9EKVKsYe4QpRZeeh--Yg|vPNQVCg-ocOv))GlPn|KF>qpa>Cc(c_Cti>LDK)EgCZL3w$?er!7MA8nc=Lp$J{jhqM)n&ZFw4ccb79kPEc2mDS<bN-DiDrK6K@4IMwBKqZnjd7YL)=F%%=pFXBo6*+)#7)n2;9fPqkO;6{uuKk>Nxx31Y@1D3Omg=<}7oeMl7}egMAyv5%bT}KNqYDT%<;q83S43c@86yc!fQ3mG|L&&1y{ov0cXvz74l{=C;*xtiX1}-C^I}<r{a;s_uRE8bT5Gm(|1vR;loK$bQ8}Ji+8-tAURg8!?ZqGEg>!?+DgEu{waiko6yOoN6`f8Dpn$59!$ToIR7yJcvQX7wi#igWoG^gH-g$pvG@l3!-rV=h5md*RdCV@2r|(Iu775y#M9+p8S1aKD79cx*{EqjZ-?hS*I>poOU7__dT3mXLZUzAy21*A2}7v<}?-8Q7pUDc(lyv)CWJo$IGcM#$hXBGS71{28QN#8iU)J$2?9C5S`cY|B6%N{7z*HIMqSzf=(%g7^5&p%rD|pzNk|KQc%Z-@etwbv<)#BQ_N`rBCriFQKz`m0hB1gSf!le(Y`eIP`r%Obo_`aWx0;NeolkRu_h#<V0ovm*z_^;A_L|8Sq~yA@NPw}AzvlN!T}`WCC*fKDp|#;Vl}5&yhc(Dr;lp#4p!H4N=2L6PW?aOKE~H|x{O;W|0#K>@6-_o;Q1MA#8Pa=-)PjpscJ)~T?lAI9s(E}T^kc)Q|4~wv<0cC+MM6Wz{?g+UjO47+T%282l5W)AQW5R{WWp4VSH?9%X<h8B6e&+A|Bx#>b9enID^9N$zuoNModSi0iDP{E+Dxx<G{ZQ>%}fi?#lT$PW><k+wczF-MEj#Fx{PUVKLSOJN4?}_#cUS^rAOIIHDsC<00zyChvWyU0=rPN1ppTwHm-VejDgie-QnO^Mjo#3~`z?%xNAn&~3QWz>!=Z<<uD~kpVNBx{q;6z~iyZG0tfg-l6_@r??4DcTr#>>%cqonB;UEwZ0{`@0|91?^Nms){B72PBW%Bg&`hEQ=I~*vHs~!&1Vun>dtbyiCVMiXADBVIrJN1;W^hS4@P4HUZBK0&XI)D^XUVu$7bAw&jNCRP&5s7x(eS#^dS~tUzn5MV&cM0+{X*#SVEq#5`jy}9fHE0A`!5RHDLO3r))nuEl1J{^01P+ta4g{B5UXcgsx@pU>8#0vCe5OMz3f52<G4D^nE1vH<7a_zBTB)g*mtK9CjfAJ-6{ZR^$8a^f@~2AZFa!=@h;TKRe~#!#miAM7+b<z0_<U^~1Di?&AjP?I-t`iIxW#`w;cZbl9o;F}`6ZoZg_xX~sstS#pWBxQmc;P90)7{^68Aj_ViLGZ(4TCGvp@mz^%5bG(ziLM<>9t*&xJm1|B>xR3P-PCw%iBCZoZJa6y}8X*L6&`qZ-n1J|O%!7QlxsM5Vop$3Ph9)^ZLAiU>^1f3gbo+}nBJ`nC8s5MonYrQlh_RngZ<wdVfHb^$=5#R4Dd%(gD^ohXiEwOy#|yrl$oG<cic}POMb2^WHSeQM2DL&MYQG^z2*FIuhR<8RPw$-G{>!)eJ?s6zb<Bh3e|!t@e}KI-fd"
_SIN_D = "c-j@$cYIIRAII_cxd#yhslB%twPz6d5kygh5TjOV)Gn$<MC>%w-n&7JrfHBW5d<j_d(^5Cv0@W3L-@V?@qB#mJ@=f?Ip=dm9w}g?LXLC^C9Kp7t8pi*bRwJN=OP`&l<ZPDSE&M;V*sWj5#@47vvChzZc@(NQUJE)k!~X(uXGt@+@<Gm`%1d-MLx+RzcdK#3P{oLEhr5|7|x@7A?lz%(onT9W5FimC@O6(CPfyP#*~n%m6SRorj)d}H06jw`?Au=a#F<#(&LJfy9aZ|50#{XmHEX4EXBnt)U7Jzs>W{(>eiHMU@H!xWG!h3Vrokd@ETR>NWUX%U8!|_sXznbV?je{7t-PCDUHH7#NcfsX>ennaRhCe@C~j_rE7TPO)i?z2FaM}BXw*^OceE%`XLQ{TGQV*k`MmCur|^YjA_d|b|G6ksWIk#D=kBMd-}(ij(kTtQv7J!i8h@X6SDYIhP+*<hj1K3I(BrW4wB&3P0HR~x`7crDZ?$)`A+JIUGV88^~I>((ld<hBQ5VI)eB_5pcTfU-uKcssMnvh93Z(2WZXzcwjWqqTt(5rQVX~Zm39x4ZerI+>DVZ#>uA;<&+z(-F;dH)q_4(G)zJWn827VOb3A(+V<t!+P;(;V##nrSXAtoxF>ky@o?zyL(+HbPKiG||A*=y<<0c}eNC}9T%6G&e3Ad+7rKZz2l2K}gv>hifb|yK6%PjiE>rm;(Ul2xp+<=|UGY-vR4rnl!cWi;@d}1wToY=fX`izjJ>}hO9EW(#@&X<!zj9o$5O6eucDycS7;JKQZYse>#;@lT&r5@`k$0np;=5Nwn+((0O_9Mda79}^Zh8Tw}$b{EM>TRMuZbtBJGkX9Dcn{Al)Wru}+{*hlsn+kT9b(}`l5@<#Z-~VW*gq&kDq2Qy$Nb4T!E2;N6C0k}$pzwfaChyL^2bO!QQ|M+APH$`xr=&;LmC?VO>C^+E&Yo^dx(XWn1FJz(r82@%U<p)RNluv#Rm*NK>dT19g+$kmKI_=iX5RI{EN;<IUkAKL70J!h((rT^pDkeiH%9zgU7l1up6yUNZqgsuMmGy3OdC%c%7Dh!Z=Jn!##!1@HoqNq{8K#RQEjh{RQfw!M}_Z^Da_;N%FodmA=BgbCvyrMBMoD{2%w|HO@HN+>rcH`6h7@2k%>)bp&EQyl-=c@31#;0;zDnE9FCBq@wse@_|^?z0X)M7>P(homBclIR1JdJw*10?7>Hz-N*c$!F#wrVgI4-Q~Jkyd_eFs&en6rh{i9tGtm!0h(Uvw+_jjB*mU;qE85@+^1db>ro3Sv!}Bfe;Q5X*BOU(l*|WHs!9DPSal!i|ePvSjvvdmmWfb{^F}jYQ9HVK7Ln2O~kTqI`qFIc*(Fq~=3yrfHjmFQ}XoGur<HB=xqam(F6>{*7Dmje?BLc^91DWv3W%LL+-Hi6<Had^r^H4W0vE3=d%CC&-=QFyJpIGQnfOZ9qwqZzNqhL(KHY_N@Sc)3C6f<&zABJE8LXn9)#fe?QXaM5SxTMikOen?pO4BET$`~c%U|A#gaz>u0TAmo#kE2Nb@>~HGiRD3iEWvBMLH$ZbyHKez?cs$$B*VRm(OCFZH44Drh{KyNzh4{msm8oeth!Ngcz;oYGOR-;y3}NjwT#Lm1-1_5@TzOnuO6`wT%Y<V*T5*RC;g#JW7e;UQ4ufd!8B!VNWoffqm9js;_<pUx%MGm3v$$wU(CcC^k`-D!Pm&OHFLoN#KY?w_DvhUp<Y}1`Jx@|kO}iG`>8$o#%ct1FuH&)9T_VYU_Yw)8Tldr)jAm!>})h1GqE2}5#?``rHfJPu147c$PZk4kV~AwE41lp)ayIO-^=J7e0v)Wg-ajy39g`iU!yQAMI_w%F-9~Fq%Xu{S%0HbNQKt`=7)fR?1MqX`GGovjXn=G8aAAHAnQn@#n^<ZKe9g&Jc_<X8(l^G7^4Ja{=}a7+2|xPv1lBz(P2DmHG%VoTS0t-+a#k_m=(-BJVK0`p#y^PD?FyKzEi1(aC}CYX_#)*9$|<;971N$HwMil|Ff7Yyh54RFRU;2<1(J2XBcBf`Pr-+t|4R&dj@tc`v~LFVjg*h%Y38tID$+JTwpXDAq(jrQP{J{=osE2eld3kiY_s#h_)CCpQXe^AN0j_bX-m>JcZW^Vqxe?qiDFUqCJKq7#C1xHFL)}1mPdt!CQ=9!+yselv~F;!VrZD>&d|eqjsBU3-1W#f&!b3nj;>?w$L7L5wVr~0(KkY!X?~6&Pbzt2tfqKMbY+8qqBG$P5E}-k+jojLk#`wGMbBCf3tqOjk?EjFX9~*?KQfK6>;n>yu-@<{G$H>VqrGcVmA^GvYv;G7UBV(qxfN?L+F;kd@%AK=6%Gd;ZgcS)<nwScZ}~z<nRRXPm+sM%;U6CjkD|*^h5~OBOLS3k>~S98EANcw%B`#`CKMGGEn^rbH)0r#6-jYxDPS+I%g5dn0UkJAxhn(K0+}(ne&&zIloOCJVWU_+zse}C^WoB-tTkfu`rdk4{3+$kBmk<CJut1aQ9#@ay+H{8Ff)TjlGj@)D=m{!1Pz_bzFhhYtF<Q!{1xx^Oo`9Gs?bW%`i2CI|q9{7|qBezi9M{y#c?^Mvieb58peE8d*m^SsaZ-EIhJ0@`QIbM_p0U#nD=9gm-pFM_nByVPFnN6LUHW#c^byVlGEDFc|SjMh22{(>9NzL&!j^yQ7N;{)&F{IeLH+1*n5fC{fT+X{<*qqWS0hGO`w?KBgfWK1CeeLaw6BqnM-Kh{kg)D9$`e&?l~-Oi4%n*jvicJ>)9wr~+=HSs6!L5sd<69esl+WR!Dc%R8!u4wwj!ijGDh0@j21R-zwFK!?hXW+4J8$VAVo^!>G?Fucd!>W+?}Sq(>LajmAK{IwiSN3q)EppK(Hn26tSrY^atM;x5Ntop=4NCWzANWYkZ?u~eE>?i>@@V<#7x2BHV;eoE`jk!p`p#M?7nWMWX(VTv<(8tlb7LKmB<h>Pq8CPr85*Y~j#!=2Tjv|qOkhYHIx8oi0SoJMAZO_=@+JQAgY)AG1-lLQs`vxJMSX;y+wzH!|$e;euyQ`z$(Ivo9%WjMlt9m$k1pl7I#yR-(ax?_7sM4F~KGZ>YKSvi38tCXY9>ILickDnsru27IXaN184a^`%OA(F2KhPdo2eU@-9>REW3TM7N4<!c14rAW9Go1Gktm}{D4UgeJ+R;Ar97BC{`HAwetijKY4v%+q0x1|df$<^*H71fTJjALX`of+`%oT3I#K&_?oXi^GCWeL(A2ab7AE&Tarcwt!(;T(JaI8lpZlUjV`bFSOM?Z!#&tDw*BIJuO>S5|^ax#ZA^(%4UI+uK*CPv~MR?l<v58BLUKf-kZb4JaD<P5zQu?CBo*Ahp52*Et;L!+g{S;m?#C(a7a<4Q-L(PR~86*bq;2A!~Gt)t@W7}sx(nuj~;zk$67_l=I8!FLmTAJro`FBq|zGqQ!>ZN$WY-}x5lXxSg+2cc2q7$g4VoZ|6z_UsNvRnZM`*oQeWj%xiydnBRQF53Lf9zwI-)Y;>xajc`2dl_>acg8+PixG#;`yKT|5K`cAfPA1IQqVe{Zzyw+J&t`ie8^Eif}_0uaMxlh3LW7rBkU-3a0nF>xsP!X_l^-O$<YS{9%n4rfWQ-s33E@fS5Wa3{hj74VB{J0!&%~;qyBmNLNx9`|1t(_L9L7YeZW4Xq1GkNB0ggIWqwia3S&b4|HvVhW6(8nfW_BodxKoV<)))aSc;Eml1zS)iie21#T}5sS|HnP@`|>Yi_kla<1XzHbkETWY{c~Y#K))y>{rZr$o++Vk12aXPM&g>pOL?pjOi6;^EKX3j)L#-o;Al{xMvXmgQFH7X`jiy`@|VU9Kt_yE>T%lNyd^QtFSCqeY0BS%4W3~Yf;k0sv-uWadxY0u2#xv)fIhEDYsQi6v$&0glc)Md@&D;5r+$Kb+`H-+M+8ie?@=!tfKN;tt?=5s-TryA^L=0VXO2aR_;ZqTinW}gw=ngtiCQo4D7@8vQ~A<S+yz8JI<qd1*>Bf8H)$wti*3+tJXMC#j06Vt1d`Kldr9&;|a=Fx2lEih(SON%m02nV^~eAG1!DBwNS^()@7XatO_9z|6*T#Vm2T?8a1?PiU?0D|3-XAT4SpUO|0(0x2e^~W>()ex7vX;H1V;Tgqba@Mzx|Ghw&0ce61$8w)}k}Pk4!XZLPj<N4;;Y67du+?a2vTIxs%8!ziSoc1P-CINbcKmS6`;cd{CQojBf^weu%W7>a9MtO|Bzj)+2!0NP^*c6MVuyYmeTumJ~pSk3QgbqV3$Q4iI6S<OJ}-d1NZx{uXWobGGYtsiv)t@3?O{*i)vu>Gy_BMs{Z&=)=->p*gfGK2WV1>8fwABZ!AIgYSuf!RpLN9-I)-hQ+yfl)YtrlYO)jI}!QGr5GvIM#kV<Cs7UT*iutd`Imd-cf9lRdM*@WiaEJ%$iN%Okg!?PbEjlHjOb%x4MsJGsqVX;KfX<nzN{nBcWEa!YIdmw4Gy>fE>SCWtq$RAvX%3ApGZ9{feCP8T&%3U$6zIaT%$|y~xT30eFG#ORR1q6@``(dl}Dof$x@E4Z;$n!(%1ujiswNV~AQ!Y(%ahUnskld0_53_6R<$XB@v-ZO8w@=@+3Ww1M-8w&;Y7IIxjD5Mgx(O*az{PqBOpb3o;-tUF#{@;3Gc21U{?ikN>|Jwi-0?>npxA{pIxat@F`hP{AcIDnYH=zkY+a1Mk1<{J{wdpCVz9rokQ9_qzf{eew!R*x}fztu@pJV1MFK1iR38561|5DR}G9ku?UF9hHk)*K=3QT8JSBywlqAu1eWjz~w<B+mD7@^^x=lU7$S`4o3AQjvx>r&$NwKqd;FVSUgP{@9OC7<`uaNI%coUSw_1^b+?UdLjKXaj!BSMEplARJg`7mS1O`@f59YaK9(BSJ4w*x44IK@izPK4)1rZ+9Mc#_sBmovFSeFQrQm=7!MvkwCewea^!eyRSHWGjue!CLVX;-U9@|~U4e(sDSttXG{%?CcQk#)o$`jYeaGJ$3`HnTzbA(o%ny+tSg((a1q(B+3V)&uSFrK3)&BwGJxL<"


def build_program(upto="p5", dbg=False):
    PH = ["p0", "p1", "p2", "p3", "p4", "p5"].index(upto)
    nc = bass.Bass("TRN2", target_bir_lowering=False, detect_race_conditions=False)

    x0_h = nc.dram_tensor("x0", [S, D], F32, kind="ExternalInput")
    wl_h = nc.dram_tensor("wl", [128, NDC], F32, kind="ExternalInput")
    wr_h = nc.dram_tensor("wr", [128, NDC], F32, kind="ExternalInput")
    mask_h = nc.dram_tensor("maskb", [S, S], F32, kind="ExternalInput")
    pos_h = nc.dram_tensor("posb", [S, 1], I32, kind="ExternalInput")
    wlr_h = nc.dram_tensor("wlr", [1, D], F32, kind="ExternalInput")
    wrr_h = nc.dram_tensor("wrr", [1, D], F32, kind="ExternalInput")
    rwb_h = nc.dram_tensor("rwb", [128, D], F32, kind="ExternalInput")
    rb_h = nc.dram_tensor("rb", [128, 1], F32, kind="ExternalInput")
    wq_h = nc.dram_tensor("wq", [D, D], F32, kind="ExternalInput")
    wk_h = nc.dram_tensor("wk", [D, D], F32, kind="ExternalInput")
    wv_h = nc.dram_tensor("wv", [D, D], F32, kind="ExternalInput")
    wo_h = nc.dram_tensor("wo", [D, D], F32, kind="ExternalInput")
    w1_h = nc.dram_tensor("w1", [D, FF], F32, kind="ExternalInput")
    w2_h = nc.dram_tensor("w2", [FF, D], F32, kind="ExternalInput")
    ln1b_h = nc.dram_tensor("ln1b", [128, D], F32, kind="ExternalInput")
    ln2b_h = nc.dram_tensor("ln2b", [128, D], F32, kind="ExternalInput")
    ident_h = nc.dram_tensor("IDENT", [128, 128], BF16, kind="ExternalInput")
    ones_h = nc.dram_tensor("ONES", [128, 128], BF16, kind="ExternalInput")
    ltri_h = nc.dram_tensor("LTRI", [128, 128], BF16, kind="ExternalInput")
    iotaf_h = nc.dram_tensor("IOTA_F", [128, 512], F32, kind="ExternalInput")
    iotapf_h = nc.dram_tensor("IOTA_P_F", [128, 1], F32, kind="ExternalInput")
    iotapb_h = nc.dram_tensor("IOTA_P_B", [128, 1], BF16, kind="ExternalInput")
    chunkv_h = nc.dram_tensor("CHUNKV_B", [128, 32], BF16, kind="ExternalInput")
    cost_h = nc.dram_tensor("COS_T", [S, D], F32, kind="ExternalInput")
    sints_h = nc.dram_tensor("SIN_TS", [S, D], F32, kind="ExternalInput")

    out_h = nc.dram_tensor("out", [S, D], F32, kind="ExternalOutput")
    xn_tm_h = nc.dram_tensor("XN_TM", [S, D], BF16)
    eh_h = nc.dram_tensor("EH", [S, S], BF16)
    x1s_h = nc.dram_tensor("X1S", [S, D], F32, kind="ExternalOutput") if dbg else nc.dram_tensor("X1S", [S, D], F32)
    grow_h = nc.dram_tensor("GROW", [1, D], F32)
    xnlo_tm_h = nc.dram_tensor("XNLO_TM", [S, D], BF16)
    xntlo_h = nc.dram_tensor("XNTLO", [D, S], BF16)
    ws_h = nc.dram_tensor("WS", [S, 1], F32, kind="ExternalOutput") if dbg else nc.dram_tensor("WS", [S, 1], F32)
    dbg_h = nc.dram_tensor("DBG", [128, 64], F32, kind="ExternalOutput") if dbg else None

    g = ExitStack()
    sb = lambda name, shp, dt: g.enter_context(nc.sbuf_tensor(name, shp, dt))
    mksem = lambda name: Sem(g.enter_context(nc.semaphore(name)))

    with g:
        ident = sb("ident_s", [128, 128], BF16)
        ones_b = sb("ones_b", [128, 128], BF16)
        ltri = sb("ltri", [128, 128], BF16)
        iota_f = sb("iota_f_s", [128, 512], F32)
        iota_pf = sb("iota_pf", [128, 1], F32)
        iota_pb = sb("iota_pb", [128, 1], BF16)
        chunkv = sb("chunkv", [128, 32], BF16)
        rwb = sb("rwb_s", [128, D], F32)
        rb = sb("rb_s", [128, 1], F32)
        gvec = sb("gvec", [128, NDC], F32)
        zall = sb("zall", [128, NT], F32)
        invz = sb("invz", [128, NT], F32)
        wvec = sb("wvec", [128, NT], F32)
        echo_s = sb("echo_s", [128, 1], F32)
        trashS = sb("trashS", [128, D], F32)
        trashV = sb("trashV", [128, D], F32)
        stat = sb("stat", [128, 16], F32)

        d_in = mksem("d_in")
        d_cst = mksem("d_cst")
        d_xno = mksem("d_xno")
        d_xnt = mksem("d_xnt")
        d_esp = mksem("d_esp")
        d_epan = mksem("d_epan")
        d_x1w = mksem("d_x1w")
        d_ob = mksem("d_ob")
        d_ws = mksem("d_ws")
        d_g = mksem("d_g")
        p0V = mksem("p0V"); p0S = mksem("p0S"); t0T = mksem("t0T"); t0V = mksem("t0V")
        p1V = mksem("p1V"); p1Vz = mksem("p1Vz"); p1T = mksem("p1T"); p1S = mksem("p1S")
        p2T = mksem("p2T"); p2V = mksem("p2V"); p2S = mksem("p2S")
        p3V = mksem("p3V"); p3T = mksem("p3T")
        d_xlo = mksem("d_xlo")
        fxV = mksem("fxV"); fxT = mksem("fxT"); fxS = mksem("fxS"); d_fx = mksem("d_fx")
        fT = mksem("fT"); fV = mksem("fV"); fS = mksem("fS")
        aT = mksem("aT"); aV = mksem("aV"); aS = mksem("aS")
        NCONST = 11

        with nc.Block() as blk:
            stk_xn = ExitStack()
            xn = stk_xn.enter_context(nc.sbuf_tensor("xn", [128, NT * D], BF16))
            stk_m1 = ExitStack()
            xnT = [stk_m1.enter_context(nc.sbuf_tensor(f"xnT{k}", [128, S], BF16)) for k in range(NDC)]
            gmix = ExitStack()
            x0b = gmix.enter_context(nc.sbuf_tensor("x0b", [128, 2 * D], F32))
            x1buf = gmix.enter_context(nc.sbuf_tensor("x1buf", [128, 2 * D], F32))
            lnx1 = gmix.enter_context(nc.sbuf_tensor("lnx1", [128, D], F32))
            obuf = gmix.enter_context(nc.sbuf_tensor("obuf", [128, 2 * D], F32))
            stk_m1b = ExitStack()
            estrip = stk_m1b.enter_context(nc.sbuf_tensor("estrip", [128, 2 * S], BF16))
            gtb = stk_m1b.enter_context(nc.sbuf_tensor("gtb", [128, 2 * D], BF16))
            zgrp = stk_m1b.enter_context(nc.sbuf_tensor("zgrp", [128, 16], F32))
            psS = [stk_m1b.enter_context(nc.psum_tensor(f"psS{i}", [128, 512], F32)) for i in range(2)]
            psX = [stk_m1b.enter_context(nc.psum_tensor(f"psX{i}", [128, 1024], BF16)) for i in range(2)]
            xnf = stk_m1b.enter_context(nc.sbuf_tensor("xnf", [128, D], F32))
            xnlo = stk_m1b.enter_context(nc.sbuf_tensor("xnlo", [128, 2 * D], BF16))
            xlo_st = stk_m1b.enter_context(nc.sbuf_tensor("xlo_st", [128, 2 * D], BF16))

            @blk.sync
            def _(sp):
                for dst, src in [
                    (ident, ident_h), (ones_b, ones_h), (ltri, ltri_h), (iota_f, iotaf_h),
                    (iota_pf, iotapf_h), (iota_pb, iotapb_h), (chunkv, chunkv_h),
                    (rwb, rwb_h), (rb, rb_h),
                ]:
                    d_cst.inc(sp.dma_start(dst[:], src[:]), 16)
                d_cst.inc(sp.dma_start(obuf[:, 0:NDC], wl_h[:]), 16)
                d_cst.inc(sp.dma_start(obuf[:, NDC:2 * NDC], wr_h[:]), 16)
                for t in range(NT):
                    if t >= 2:
                        sp.wait_ge(p0V.h, 4 * (t - 2) + 4)
                    d_in.inc(sp.dma_start(x0b[:, (t % 2) * D:(t % 2 + 1) * D],
                                          x0_h[128 * t:128 * (t + 1), :]), 16)
                    if t >= 2:
                        ts_ = t - 2
                        d_xno.inc(sp.dma_start(xn_tm_h[128 * ts_:128 * (ts_ + 1), :],
                                               xn[:, ts_ * D:(ts_ + 1) * D]), 16)
                        d_xno.inc(sp.dma_start(xnlo_tm_h[128 * ts_:128 * (ts_ + 1), :],
                                               xnlo[:, (ts_ % 2) * D:(ts_ % 2 + 1) * D]), 16)
                for ts_ in [NT - 2, NT - 1]:
                    sp.wait_ge(p0V.h, 4 * ts_ + 4)
                    d_xno.inc(sp.dma_start(xn_tm_h[128 * ts_:128 * (ts_ + 1), :],
                                           xn[:, ts_ * D:(ts_ + 1) * D]), 16)
                    d_xno.inc(sp.dma_start(xnlo_tm_h[128 * ts_:128 * (ts_ + 1), :],
                                           xnlo[:, (ts_ % 2) * D:(ts_ % 2 + 1) * D]), 16)

            @blk.scalar
            def _(s):
                for t in range(NT):
                    s.wait_ge(d_in.h, 16 * (t + 1))
                    xt = x0b[:, (t % 2) * D:(t % 2 + 1) * D]
                    o = (t % 2) * 8
                    p0S.inc(s.activation(trashS[:], xt, AF.Square, accum_out=stat[:, o + 2:o + 3]))
                    s.wait_ge(p0V.h, 4 * t + 2)
                    s.activation(stat[:, o + 5:o + 6], stat[:, o + 4:o + 5], AF.Ln)
                    s.drain()
                    p0S.inc(s.activation(stat[:, o + 5:o + 6], stat[:, o + 5:o + 6], AF.Exp, scale=-0.5))
                    if t >= 2:
                        ts_ = t - 2
                        s.wait_ge(t0V.h, 16 * ts_ + 16)
                        d_xlo.inc(s.dma_start(
                            bass.AP(xntlo_h, 128 * ts_, [[S, 128], [S * 128, NDC], [1, 128]]),
                            xlo_st[:, (ts_ % 2) * D:(ts_ % 2 + 1) * D], allow_slow_non_contiguous=True), 16)
                for ts_ in [NT - 2, NT - 1]:
                    s.wait_ge(t0V.h, 16 * ts_ + 16)
                    d_xlo.inc(s.dma_start(
                        bass.AP(xntlo_h, 128 * ts_, [[S, 128], [S * 128, NDC], [1, 128]]),
                        xlo_st[:, (ts_ % 2) * D:(ts_ % 2 + 1) * D], allow_slow_non_contiguous=True), 16)

            @blk.tensor
            def _(t):
                t.wait_ge(d_cst.h, 16 * NCONST)
                for ti in range(NT):
                    t.wait_ge(p0V.h, 4 * ti + 3)
                    for k in range(NDC):
                        j = 16 * ti + k
                        if j >= 2:
                            t.wait_ge(t0V.h, j - 1)
                        t0T.inc(t.transpose(psX[j % 2][:, 0:128], xn[:, ti * D + 128 * k:ti * D + 128 * (k + 1)],
                                            ident[:]))
                    t.wait_ge(p0V.h, 4 * ti + 4)
                    for k in range(NDC):
                        j = 16 * ti + 8 + k
                        t.wait_ge(t0V.h, j - 1)
                        t0T.inc(t.transpose(psX[j % 2][:, 0:128],
                                            xnlo[:, (ti % 2) * D + 128 * k:(ti % 2) * D + 128 * (k + 1)],
                                            ident[:]))

            @blk.vector
            def _(v):
                def evacs(ti):
                    for k in range(NDC):
                        j = 16 * ti + k
                        v.wait_ge(t0T.h, j + 1)
                        t0V.inc(v.tensor_copy(xnT[k][:, 128 * ti:128 * (ti + 1)], psX[j % 2][:, 0:128]))
                    if ti >= 2:
                        v.wait_ge(d_xlo.h, 16 * (ti - 1))
                    for k in range(NDC):
                        j = 16 * ti + 8 + k
                        v.wait_ge(t0T.h, j + 1)
                        t0V.inc(v.tensor_copy(xlo_st[:, (ti % 2) * D + 128 * k:(ti % 2) * D + 128 * (k + 1)],
                                              psX[j % 2][:, 0:128]))
                v.wait_ge(d_cst.h, 16 * NCONST)
                v.tensor_tensor(gvec[:], obuf[:, 0:NDC], obuf[:, NDC:2 * NDC], op=ALU.mult)
                for t in range(NT):
                    v.wait_ge(d_in.h, 16 * (t + 1))
                    xt = x0b[:, (t % 2) * D:(t % 2 + 1) * D]
                    o = (t % 2) * 8
                    v.reduce_sum(stat[:, o:o + 1], xt, axis=AXX)
                    v.drain()
                    p0V.inc(v.tensor_scalar(stat[:, o + 1:o + 2], stat[:, o:o + 1], 1.0 / D, None, op0=ALU.mult))
                    v.drain()
                    v.wait_ge(p0S.h, 2 * t + 1)
                    v.tensor_tensor(stat[:, o + 3:o + 4], stat[:, o + 1:o + 2], stat[:, o + 1:o + 2], op=ALU.mult)
                    v.drain()
                    v.scalar_tensor_tensor(stat[:, o + 4:o + 5], stat[:, o + 2:o + 3], 1.0 / D,
                                           stat[:, o + 3:o + 4], op0=ALU.mult, op1=ALU.subtract)
                    v.drain()
                    p0V.inc(v.tensor_scalar(stat[:, o + 4:o + 5], stat[:, o + 4:o + 5], 1e-5, None, op0=ALU.add))
                    v.wait_ge(p0S.h, 2 * t + 2)
                    v.tensor_scalar(xnf[:], xt,
                                    stat[:, o + 1:o + 2], stat[:, o + 5:o + 6],
                                    op0=ALU.subtract, op1=ALU.mult)
                    v.drain()
                    p0V.inc(v.tensor_copy(xn[:, t * D:(t + 1) * D], xnf[:]))
                    v.drain()
                    if t >= 2:
                        v.wait_ge(d_xno.h, 32 * (t - 1))
                    p0V.inc(v.tensor_tensor(xnlo[:, (t % 2) * D:(t % 2 + 1) * D], xnf[:],
                                            xn[:, t * D:(t + 1) * D], op=ALU.subtract))
                    if t >= 1:
                        evacs(t - 1)
                evacs(NT - 1)

            if PH >= 1:
                @blk.vector
                def _(v):
                    v.wait_ge(t0V.h, 16 * NT)
                    for tau in range(NT):
                        if tau >= 2:
                            v.wait_ge(p1T.h, 8 * (tau - 1))
                        for k in range(NDC):
                            ins = v.tensor_scalar(
                                gtb[:, (tau % 2) * D + 128 * k:(tau % 2) * D + 128 * (k + 1)],
                                xnT[k][:, 128 * tau:128 * (tau + 1)],
                                gvec[:, k:k + 1], None, op0=ALU.mult)
                        p1V.inc(ins)
                        if tau >= 1:
                            v.wait_ge(p1S.h, 8 * tau)
                            p1Vz.inc(v.reduce_sum(zall[:, tau - 1:tau],
                                                  zgrp[:, ((tau - 1) % 2) * 8:((tau - 1) % 2) * 8 + 8],
                                                  axis=AXX))
                    v.wait_ge(p1S.h, 8 * NT)
                    p1Vz.inc(v.reduce_sum(zall[:, NT - 1:NT],
                                          zgrp[:, ((NT - 1) % 2) * 8:((NT - 1) % 2) * 8 + 8], axis=AXX))

                @blk.tensor
                def _(t):
                    for tau in range(NT):
                        t.wait_ge(p1V.h, tau + 1)
                        for grp in range(8):
                            G = 8 * tau + grp
                            if G >= 2:
                                t.wait_ge(p1S.h, G - 1)
                            for k in range(NDC):
                                ins = t.matmul(psS[G % 2][:],
                                               gtb[:, (tau % 2) * D + 128 * k:(tau % 2) * D + 128 * (k + 1)],
                                               xnT[k][:, 512 * grp:512 * (grp + 1)],
                                               start=(k == 0), stop=(k == NDC - 1))
                            p1T.inc(ins)

                @blk.scalar
                def _(s):
                    for tau in range(NT):
                        for grp in range(8):
                            G = 8 * tau + grp
                            s.wait_ge(p1T.h, G + 1)
                            if grp == 0 and tau >= 2:
                                s.wait_ge(d_esp.h, 16 * (tau - 1))
                                s.wait_ge(p1Vz.h, tau - 1)
                            p1S.inc(s.activation(
                                estrip[:, (tau % 2) * S + 512 * grp:(tau % 2) * S + 512 * (grp + 1)],
                                psS[G % 2][:], AF.Exp,
                                accum_out=zgrp[:, (tau % 2) * 8 + grp:(tau % 2) * 8 + grp + 1]))

                @blk.sync
                def _(sp):
                    for tau in range(NT):
                        sp.wait_ge(p1S.h, 8 * (tau + 1))
                        d_esp.inc(sp.dma_start(eh_h[128 * tau:128 * (tau + 1), :],
                                               estrip[:, (tau % 2) * S:(tau % 2) * S + S]), 16)

            stk_m1b.close()

            if PH >= 2:
                stk_m2 = ExitStack()
                psAV = [stk_m2.enter_context(nc.psum_tensor(f"psAV{i}", [128, 512], F32)) for i in range(4)]
                pan = stk_m2.enter_context(nc.sbuf_tensor("pan", [128, 2 * S], BF16))

                @blk.scalar
                def _(s):
                    s.wait_ge(d_esp.h, 16 * NT)
                    s.wait_ge(p1Vz.h, NT)
                    s.activation(invz[:], zall[:], AF.Ln)
                    s.drain()
                    p2S.inc(s.activation(invz[:], invz[:], AF.Exp, scale=-1.0))
                    for sig in range(NT):
                        o = (sig % 2)
                        so = o * 8
                        s.wait_ge(p2V.h, 3 * sig + 1)
                        p2S.inc(s.activation(trashS[:], x1buf[:, o * D:(o + 1) * D], AF.Square,
                                             accum_out=stat[:, so + 2:so + 3]))
                        s.wait_ge(p2V.h, 3 * sig + 2)
                        s.activation(stat[:, so + 5:so + 6], stat[:, so + 4:so + 5], AF.Ln)
                        s.drain()
                        p2S.inc(s.activation(stat[:, so + 5:so + 6], stat[:, so + 5:so + 6],
                                             AF.Exp, scale=-0.5))

                @blk.sync
                def _(sp):
                    sp.wait_ge(d_esp.h, 16 * NT)
                    for sig in range(NT):
                        if sig >= 2:
                            sp.wait_ge(p2T.h, sig - 1)
                        d_epan.inc(sp.dma_start(pan[:, (sig % 2) * S:(sig % 2 + 1) * S],
                                                bass.AP(eh_h, 128 * sig, [[S, 128], [S * 128, NT], [1, 128]])),
                                   16)
                        if sig >= 2:
                            sp.wait_ge(p2V.h, 3 * (sig - 2) + 1)
                        d_in.inc(sp.dma_start(x0b[:, (sig % 2) * D:(sig % 2 + 1) * D],
                                              x0_h[128 * sig:128 * (sig + 1), :]), 16)
                        if sig >= 2:
                            sg = sig - 2
                            og = sg % 2
                            sp.wait_ge(p2V.h, 3 * sg + 1)
                            d_x1w.inc(sp.dma_start(x1s_h[128 * sg:128 * (sg + 1), :],
                                                   x1buf[:, og * D:(og + 1) * D]), 16)
                            sp.wait_ge(p2V.h, 3 * sg + 3)
                            d_ob.inc(sp.dma_start(out_h[128 * sg:128 * (sg + 1), :],
                                                  obuf[:, og * D:(og + 1) * D]), 16)
                    for sg in [NT - 2, NT - 1]:
                        og = sg % 2
                        sp.wait_ge(p2V.h, 3 * sg + 1)
                        d_x1w.inc(sp.dma_start(x1s_h[128 * sg:128 * (sg + 1), :],
                                               x1buf[:, og * D:(og + 1) * D]), 16)
                        sp.wait_ge(p2V.h, 3 * sg + 3)
                        d_ob.inc(sp.dma_start(out_h[128 * sg:128 * (sg + 1), :],
                                              obuf[:, og * D:(og + 1) * D]), 16)
                    sp.wait_ge(p2V.h, 3 * NT)
                    d_ws.inc(sp.dma_start(bass.AP(ws_h, 0, [[1, 128], [128, NT]]), wvec[:], allow_slow_non_contiguous=True), 16)

                IN2 = 16 * NT

                @blk.tensor
                def _(t):
                    for sig in range(NT):
                        t.wait_ge(d_epan.h, 16 * (sig + 1))
                        if sig >= 2:
                            t.wait_ge(p2V.h, 3 * (sig - 2) + 1)
                        for tau in range(NT):
                            for hf in range(2):
                                ins = t.matmul(psAV[(sig % 2) * 2 + hf][:],
                                               pan[:, (sig % 2) * S + 128 * tau:(sig % 2) * S + 128 * (tau + 1)],
                                               xn[:, tau * D + 512 * hf:tau * D + 512 * (hf + 1)],
                                               start=(tau == 0), stop=(tau == NT - 1))
                        p2T.inc(ins)

                @blk.vector
                def _(v):
                    for sig in range(NT):
                        o = (sig % 2)
                        so = o * 8
                        v.wait_ge(p2T.h, sig + 1)
                        v.wait_ge(d_in.h, IN2 + 16 * (sig + 1))
                        if sig == 0:
                            v.wait_ge(p2S.h, 1)
                        if sig >= 2:
                            v.wait_ge(d_x1w.h, 16 * (sig - 1))
                            v.wait_ge(d_ob.h, 16 * (sig - 1))
                        for hf in range(2):
                            ins = v.scalar_tensor_tensor(
                                x1buf[:, o * D + 512 * hf:o * D + 512 * (hf + 1)],
                                psAV[o * 2 + hf][:], invz[:, sig:sig + 1],
                                x0b[:, o * D + 512 * hf:o * D + 512 * (hf + 1)],
                                op0=ALU.mult, op1=ALU.add)
                        p2V.inc(ins)
                        v.drain()
                        x1t = x1buf[:, o * D:(o + 1) * D]
                        v.reduce_sum(stat[:, so:so + 1], x1t, axis=AXX)
                        v.drain()
                        v.tensor_scalar(stat[:, so + 1:so + 2], stat[:, so:so + 1], 1.0 / D, None, op0=ALU.mult)
                        v.drain()
                        v.wait_ge(p2S.h, 1 + 2 * sig + 1)
                        v.tensor_tensor(stat[:, so + 3:so + 4], stat[:, so + 1:so + 2],
                                        stat[:, so + 1:so + 2], op=ALU.mult)
                        v.drain()
                        v.scalar_tensor_tensor(stat[:, so + 4:so + 5], stat[:, so + 2:so + 3], 1.0 / D,
                                               stat[:, so + 3:so + 4], op0=ALU.mult, op1=ALU.subtract)
                        v.drain()
                        p2V.inc(v.tensor_scalar(stat[:, so + 4:so + 5], stat[:, so + 4:so + 5],
                                                1e-5, None, op0=ALU.add))
                        v.wait_ge(p2S.h, 1 + 2 * sig + 2)
                        v.tensor_scalar(lnx1[:], x1t,
                                        stat[:, so + 1:so + 2], stat[:, so + 5:so + 6],
                                        op0=ALU.subtract, op1=ALU.mult)
                        v.drain()
                        v.tensor_tensor(obuf[:, o * D:(o + 1) * D], x1t, lnx1[:], op=ALU.add)
                        v.tensor_tensor(trashV[:], lnx1[:], rwb[:], op=ALU.mult)
                        v.drain()
                        v.reduce_sum(stat[:, so + 6:so + 7], trashV[:], axis=AXX)
                        v.drain()
                        p2V.inc(v.tensor_tensor(wvec[:, sig:sig + 1], stat[:, so + 6:so + 7], rb[:], op=ALU.add))

                stk_m2.close()
                gmix.close()
            if PH >= 3:
                _emit_fixup(nc, blk, locals())
            if PH < 1:
                stk_m1b.close()
            if PH < 2:
                gmix.close()
            stk_m1.close()
            stk_xn.close()

            if PH >= 3:
                stk_sel = ExitStack()
                idx_i = stk_sel.enter_context(nc.sbuf_tensor("idx_i", [128, NB], I32))
                idxs_i = stk_sel.enter_context(nc.sbuf_tensor("idxs_i", [128, NB], I32))
                cnt_all = stk_sel.enter_context(nc.sbuf_tensor("cnt_all", [128, 1], F32))
                fstk = ExitStack()
                ln1b = fstk.enter_context(nc.sbuf_tensor("ln1b_s", [128, D], F32))
                ln2b = fstk.enter_context(nc.sbuf_tensor("ln2b_s", [128, D], F32))
                x1sel = fstk.enter_context(nc.sbuf_tensor("x1sel", [128, NB * D], F32))
                xsel = fstk.enter_context(nc.sbuf_tensor("xsel", [128, NB * D], F32))
                x1f = fstk.enter_context(nc.sbuf_tensor("x1f", [128, NB * D], F32))
                wselv = fstk.enter_context(nc.sbuf_tensor("wselv", [128, NB], F32))
                poss = fstk.enter_context(nc.sbuf_tensor("poss", [128, NB], I32))
                mselT = fstk.enter_context(nc.sbuf_tensor("mselT", [128, NB * 512], F32))
                psT = [fstk.enter_context(nc.psum_tensor(f"psT{i}", [128, 1024], BF16)) for i in range(2)]
                psM = [fstk.enter_context(nc.psum_tensor(f"psM{i}", [128, 512], F32)) for i in range(2)]
                psQ = [fstk.enter_context(nc.psum_tensor(f"psQ{i}", [128, 512], F32)) for i in range(2)]
                gsel2 = ExitStack()
                pchunk = gsel2.enter_context(nc.sbuf_tensor("pchunk", [128, NT * 512], BF16))

                p3s = ExitStack()
                p3b = lambda name, shp, dt: p3s.enter_context(nc.sbuf_tensor(name, shp, dt))
                lo = p3b("lo", [128, 1], F32)
                hi = p3b("hi", [128, 1], F32)
                mid = p3b("mid", [128, 1], F32)
                cmpb = p3b("cmpb", [128, 32], F32)
                cntb = p3b("cntb", [128, 1], BF16)
                cond = p3b("cond", [128, 1], F32)
                tmp1 = p3b("tmp1", [128, 1], F32)
                selm = p3b("selm", [128, 32], F32)
                cntp = p3b("cntp", [128, 1], F32)
                ra = p3b("ra", [128, 32], F32)
                rbb = p3b("rbb", [128, 32], F32)
                rank = p3b("rank", [128, 32], F32)
                idxf = p3b("idxf", [128, NB], F32)
                psW = p3s.enter_context(nc.psum_tensor("psW", [128, 16], F32))
                psC = psW[:, 0:1]
                psO = psW[:, 1:2]
                psI = psW[:, 8:16]

                @blk.vector
                def _(v):
                    v.wait_ge(p2V.h, 3 * NT)
                    v.wait_ge(d_ob.h, 16 * NT)
                    v.wait_ge(d_x1w.h, 16 * NT)
                    v.wait_ge(d_ws.h, 32)
                    v.reduce_max(tmp1[:], wvec[:], axis=AXX, apply_absolute_value=True)
                    v.drain()
                    p3V.inc(v.tensor_copy(cntb[:], tmp1[:]))
                    v.wait_ge(p3T.h, 1)
                    v.tensor_scalar(hi[:], psC, 1.0, None, op0=ALU.add)
                    v.drain()
                    p3V.inc(v.tensor_scalar(lo[:], hi[:], -1.0, None, op0=ALU.mult))
                    v.drain()
                    for it in range(NITER):
                        v.tensor_tensor(mid[:], lo[:], hi[:], op=ALU.add)
                        v.drain()
                        v.tensor_scalar(mid[:], mid[:], 0.5, None, op0=ALU.mult)
                        v.drain()
                        v.tensor_scalar(cmpb[:], wvec[:], mid[:, 0:1], None, op0=ALU.is_gt)
                        v.drain()
                        v.reduce_sum(cntp[:], cmpb[:], axis=AXX)
                        v.drain()
                        p3V.inc(v.tensor_copy(cntb[:], cntp[:]))
                        v.wait_ge(p3T.h, 2 + it)
                        v.tensor_scalar(cond[:], psC, 511.5, None, op0=ALU.is_gt)
                        v.tensor_tensor(tmp1[:], mid[:], lo[:], op=ALU.subtract)
                        v.drain()
                        v.tensor_tensor(tmp1[:], cond[:], tmp1[:], op=ALU.mult)
                        v.drain()
                        v.tensor_tensor(lo[:], lo[:], tmp1[:], op=ALU.add)
                        v.tensor_tensor(tmp1[:], hi[:], mid[:], op=ALU.subtract)
                        v.drain()
                        v.tensor_tensor(tmp1[:], cond[:], tmp1[:], op=ALU.mult)
                        v.drain()
                        p3V.inc(v.tensor_tensor(hi[:], mid[:], tmp1[:], op=ALU.add))
                        v.drain()
                    v.tensor_scalar(selm[:], wvec[:], hi[:, 0:1], None, op0=ALU.is_gt)
                    v.drain()
                    v.reduce_sum(cntp[:], selm[:], axis=AXX)
                    v.drain()
                    p3V.inc(v.tensor_copy(cntb[:], cntp[:]))
                    v.tensor_copy(ra[:], selm[:])
                    v.drain()
                    cur, nxt = ra, rbb
                    sh = 1
                    while sh < 32:
                        v.tensor_copy(nxt[:, 0:sh], cur[:, 0:sh])
                        v.tensor_tensor(nxt[:, sh:32], cur[:, sh:32], cur[:, 0:32 - sh], op=ALU.add)
                        v.drain()
                        cur, nxt = nxt, cur
                        sh *= 2
                    v.tensor_tensor(rank[:], cur[:], selm[:], op=ALU.subtract)
                    v.wait_ge(p3T.h, NITER + 3)
                    v.tensor_copy(tmp1[:], psO)
                    v.drain()
                    v.tensor_scalar(rank[:], rank[:], tmp1[:, 0:1], None, op0=ALU.add)
                    v.drain()
                    v.tensor_copy(cnt_all[:], psC)
                    v.drain()
                    for sig in range(NT):
                        ins = v.tensor_scalar(pchunk[:, 512 * sig:512 * (sig + 1)], iota_f[:],
                                              rank[:, sig:sig + 1], selm[:, sig:sig + 1],
                                              op0=ALU.is_equal, op1=ALU.mult)
                    p3V.inc(ins)
                    v.wait_ge(p3T.h, NITER + 3 + NB)
                    for B in range(NB):
                        v.tensor_copy(tmp1[:], psM[0][:, B:B + 1])
                        v.drain()
                        v.scalar_tensor_tensor(idxf[:, B:B + 1], psM[1][:, B:B + 1], 128.0,
                                               tmp1[:], op0=ALU.mult, op1=ALU.add)
                        v.drain()
                        v.tensor_copy(idx_i[:, B:B + 1], idxf[:, B:B + 1])
                        v.tensor_tensor(tmp1[:], iota_pf[:], cnt_all[:], op=ALU.subtract)
                        v.drain()
                        v.tensor_scalar(tmp1[:], tmp1[:], float(-128 * B), None, op0=ALU.is_lt)
                        v.drain()
                        v.tensor_scalar(tmp1[:], tmp1[:], -4096.0, 4096.0, op0=ALU.mult, op1=ALU.add)
                        v.drain()
                        v.tensor_tensor(idxf[:, B:B + 1], idxf[:, B:B + 1], tmp1[:], op=ALU.add)
                        v.drain()
                        ins = v.tensor_copy(idxs_i[:, B:B + 1], idxf[:, B:B + 1])
                    p3V.inc(ins)

                @blk.tensor
                def _(t):
                    t.wait_ge(p3V.h, 1)
                    p3T.inc(t.matmul(psC, ones_b[:], cntb[:], start=True, stop=True))
                    for it in range(NITER):
                        t.wait_ge(p3V.h, 3 + 2 * it)
                        p3T.inc(t.matmul(psC, ones_b[:], cntb[:], start=True, stop=True))
                    t.wait_ge(p3V.h, 3 + 2 * NITER)
                    p3T.inc(t.matmul(psC, ones_b[:], cntb[:], start=True, stop=True))
                    p3T.inc(t.matmul(psO, ltri[:], cntb[:], start=True, stop=True))
                    t.wait_ge(p3V.h, 4 + 2 * NITER)
                    for B in range(NB):
                        for sig in range(NT):
                            t.matmul(psM[0][:, B:B + 1],
                                     pchunk[:, 512 * sig + 128 * B:512 * sig + 128 * (B + 1)],
                                     iota_pb[:], start=(sig == 0), stop=(sig == NT - 1))
                            ins = t.matmul(psM[1][:, B:B + 1],
                                           pchunk[:, 512 * sig + 128 * B:512 * sig + 128 * (B + 1)],
                                           chunkv[:, sig:sig + 1], start=(sig == 0), stop=(sig == NT - 1))
                        p3T.inc(ins)

                p3s.close()

            if PH >= 4:
                env = dict(locals())
                _emit_ffn(nc, blk, env)

            if PH < 1:
                stk_m1b.close()
            if PH == 3:
                gsel2.close()
                fstk.close()
                stk_sel.close()

            if dbg and PH >= 0:
                @blk.vector
                def _(v):
                    if PH >= 1:
                        v.wait_ge(p1Vz.h, NT)
                    v.wait_ge(t0V.h, 16 * NT)
                    v.tensor_copy(trashV[:, 0:NT], zall[:])
                    v.tensor_copy(trashV[:, 32:40], xn[:, 0:8])
                    v.tensor_copy(trashV[:, 40:48], xnT[0][:, 0:8])
                    v.tensor_copy(trashV[:, 48:56], ident[:, 0:8])
                    p1Vz.inc(v.tensor_copy(trashV[:, 56:64], gtb[:, 0:8]))

                @blk.sync
                def _(sp):
                    sp.wait_ge(p1Vz.h, NT + 1)
                    d_ws.inc(sp.dma_start(dbg_h[:], trashV[:, 0:64]), 16)

            @blk.sync
            def _(sp):
                sp.wait_ge(d_xno.h, d_xno.n)
                sp.wait_ge(d_xlo.h, d_xlo.n)
                sp.wait_ge(d_xnt.h, d_xnt.n)
                sp.wait_ge(d_in.h, d_in.n)
                sp.wait_ge(d_cst.h, d_cst.n)
                sp.wait_ge(d_epan.h, d_epan.n)
                if PH >= 1:
                    sp.wait_ge(d_esp.h, d_esp.n)
                if PH >= 2:
                    sp.wait_ge(d_ob.h, d_ob.n)
                    sp.wait_ge(d_x1w.h, d_x1w.n)
                    sp.wait_ge(d_ws.h, d_ws.n)
                if PH >= 3:
                    sp.wait_ge(d_fx.h, d_fx.n)

            @blk.gpsimd
            def _(gp):
                if d_g.n:
                    gp.wait_ge(d_g.h, d_g.n)

    return nc


def _emit_fixup(nc, blk, L):
    FXN = 16
    HB = FXN * NDC
    MARGIN2 = (8e-4) ** 2
    wvec = L["wvec"]
    ones_b = L["ones_b"]; ltri = L["ltri"]; iota_f = L["iota_f"]
    iota_pf = L["iota_pf"]; iota_pb = L["iota_pb"]; chunkv = L["chunkv"]
    rwb = L["rwb"]; rb = L["rb"]; ident = L["ident"]
    xn = L["xn"]; xnT = L["xnT"]
    trashS = L["trashS"]; trashV = L["trashV"]
    x0_h = L["x0_h"]; ws_h = L["ws_h"]; xn_tm_h = L["xn_tm_h"]
    xnlo_tm_h = L["xnlo_tm_h"]; xntlo_h = L["xntlo_h"]; grow_h = L["grow_h"]
    wlr_h = L["wlr_h"]; wrr_h = L["wrr_h"]
    fxV = L["fxV"]; fxT = L["fxT"]; fxS = L["fxS"]; d_fx = L["d_fx"]; d_g = L["d_g"]
    echo_s = L["echo_s"]; iotapf_h = L["iotapf_h"]
    d_ws = L["d_ws"]; d_x1w = L["d_x1w"]; d_ob = L["d_ob"]; d_xlo = L["d_xlo"]
    d_xno = L["d_xno"]
    p2V = L["p2V"]
    P2V_END = 3 * NT

    fx = ExitStack()
    fb = lambda n, s_, d_: fx.enter_context(nc.sbuf_tensor(n, s_, d_))
    thr0 = fb("thr0", [128, 1], F32)
    flo = fb("flo", [128, 1], F32)
    fhi = fb("fhi", [128, 1], F32)
    fmid = fb("fmid", [128, 1], F32)
    fcmp = fb("fcmp", [128, 32], F32)
    fcntb = fb("fcntb", [128, 1], BF16)
    fcond = fb("fcond", [128, 1], F32)
    ftmp = fb("ftmp", [128, 1], F32)
    fcp = fb("fcp", [128, 1], F32)
    fm = fb("fm", [128, 32], F32)
    fra = fb("fra", [128, 32], F32)
    frb = fb("frb", [128, 32], F32)
    frank = fb("frank", [128, 32], F32)
    p8 = fb("p8", [128, NT * FXN], BF16)
    fidx = fb("fidx", [128, 1], F32)
    fmask8 = fb("fmask8", [128, 1], F32)
    fidxg = fb("fidxg", [128, 1], I32)
    fidxs = fb("fidxs", [128, 1], I32)
    fcnt8 = fb("fcnt8", [128, 1], F32)
    xrh = fb("xrh", [FXN, D], BF16)
    xrl = fb("xrl", [FXN, D], BF16)
    x0r = fb("x0r", [FXN, D], F32)
    grow1 = fb("grow1", [1, D], F32)
    grow8 = fb("grow8", [FXN, D], F32)
    gx = fb("gx", [FXN, D], F32)
    gxh = fb("gxh", [FXN, D], BF16)
    gxl = fb("gxl", [FXN, D], BF16)
    lhb = fb("lhb", [128, 2 * HB], BF16)
    xtl_ring = fb("xtl_ring", [128, 2 * S], BF16)
    xnl_ring = fb("xnl_ring", [128, 2 * D], BF16)
    sacc = fb("sacc", [128, FXN * NT], F32)
    sstg = fb("sstg", [128, 2 * FXN * NT], F32)
    ehT = fb("ehT", [128, FXN * NT], BF16)
    elT = fb("elT", [128, FXN * NT], BF16)
    x1r = fb("x1r", [FXN, D], F32)
    lnr8 = fb("lnr8", [FXN, D], F32)
    w8 = fb("w8", [FXN, 1], F32)
    fst = fb("fst", [FXN, 16], F32)
    psF = fx.enter_context(nc.psum_tensor("psF", [128, 16], F32))
    psG = [fx.enter_context(nc.psum_tensor(f"psG{i}", [128, 512], F32)) for i in range(2)]
    psH = [fx.enter_context(nc.psum_tensor(f"psH{i}", [128, 1024], BF16)) for i in range(2)]

    @blk.vector
    def _(v):
        v.wait_ge(p2V.h, P2V_END)
        v.wait_ge(d_ws.h, 16)
        v.wait_ge(d_ob.h, 16 * NT)
        v.wait_ge(d_x1w.h, 16 * NT)
        v.reduce_max(ftmp[:], wvec[:], axis=AXX, apply_absolute_value=True)
        v.drain()
        fxV.inc(v.tensor_copy(fcntb[:], ftmp[:]))
        v.wait_ge(fxT.h, 1)
        v.tensor_scalar(fhi[:], psF[:, 0:1], 1.0, None, op0=ALU.add)
        v.drain()
        fxV.inc(v.tensor_scalar(flo[:], fhi[:], -1.0, None, op0=ALU.mult))
        v.drain()
        for it in range(NITER):
            v.tensor_tensor(fmid[:], flo[:], fhi[:], op=ALU.add)
            v.drain()
            v.tensor_scalar(fmid[:], fmid[:], 0.5, None, op0=ALU.mult)
            v.drain()
            v.tensor_scalar(fcmp[:], wvec[:], fmid[:, 0:1], None, op0=ALU.is_gt)
            v.drain()
            v.reduce_sum(fcp[:], fcmp[:], axis=AXX)
            v.drain()
            fxV.inc(v.tensor_copy(fcntb[:], fcp[:]))
            v.wait_ge(fxT.h, 2 + it)
            v.tensor_scalar(fcond[:], psF[:, 0:1], 511.5, None, op0=ALU.is_gt)
            v.tensor_tensor(ftmp[:], fmid[:], flo[:], op=ALU.subtract)
            v.drain()
            v.tensor_tensor(ftmp[:], fcond[:], ftmp[:], op=ALU.mult)
            v.drain()
            v.tensor_tensor(flo[:], flo[:], ftmp[:], op=ALU.add)
            v.tensor_tensor(ftmp[:], fhi[:], fmid[:], op=ALU.subtract)
            v.drain()
            v.tensor_tensor(ftmp[:], fcond[:], ftmp[:], op=ALU.mult)
            v.drain()
            fxV.inc(v.tensor_tensor(fhi[:], fmid[:], ftmp[:], op=ALU.add))
            v.drain()
        v.tensor_copy(thr0[:], fhi[:])
        v.drain()
        v.tensor_scalar(fm[:], wvec[:], thr0[:, 0:1], None, op0=ALU.subtract)
        v.drain()
        v.tensor_tensor(fm[:], fm[:], fm[:], op=ALU.mult)
        v.drain()
        v.tensor_scalar(fm[:], fm[:], MARGIN2, None, op0=ALU.is_lt)
        v.drain()
        v.reduce_sum(fcp[:], fm[:], axis=AXX)
        v.drain()
        fxV.inc(v.tensor_copy(fcntb[:], fcp[:]))
        v.tensor_copy(fra[:], fm[:])
        v.drain()
        cur, nxt = fra, frb
        sh = 1
        while sh < 32:
            v.tensor_copy(nxt[:, 0:sh], cur[:, 0:sh])
            v.tensor_tensor(nxt[:, sh:32], cur[:, sh:32], cur[:, 0:32 - sh], op=ALU.add)
            v.drain()
            cur, nxt = nxt, cur
            sh *= 2
        v.tensor_tensor(frank[:], cur[:], fm[:], op=ALU.subtract)
        v.wait_ge(fxT.h, NITER + 3)
        v.tensor_copy(ftmp[:], psF[:, 1:2])
        v.drain()
        v.tensor_scalar(frank[:], frank[:], ftmp[:, 0:1], None, op0=ALU.add)
        v.drain()
        v.tensor_copy(fcnt8[:], psF[:, 0:1])
        v.drain()
        for sig in range(NT):
            ins = v.tensor_scalar(p8[:, FXN * sig:FXN * (sig + 1)], iota_f[:, 0:FXN],
                                  frank[:, sig:sig + 1], fm[:, sig:sig + 1],
                                  op0=ALU.is_equal, op1=ALU.mult)
        fxV.inc(ins)
        v.wait_ge(fxT.h, NITER + 4)
        v.tensor_copy(ftmp[0:FXN, :], psG[0][0:FXN, 0:1])
        v.drain()
        v.scalar_tensor_tensor(fidx[0:FXN, :], psG[1][0:FXN, 0:1], 128.0, ftmp[0:FXN, :],
                               op0=ALU.mult, op1=ALU.add)
        v.drain()
        v.tensor_tensor(ftmp[0:FXN, :], iota_pf[0:FXN, :], fcnt8[0:FXN, :], op=ALU.subtract)
        v.drain()
        v.tensor_scalar(fmask8[0:FXN, :], ftmp[0:FXN, :], 0.0, None, op0=ALU.is_lt)
        v.drain()
        v.tensor_tensor(fidx[0:FXN, :], fidx[0:FXN, :], fmask8[0:FXN, :], op=ALU.mult)
        v.drain()
        v.tensor_copy(fidxg[0:FXN, :], fidx[0:FXN, :])
        v.tensor_scalar(ftmp[0:FXN, :], fmask8[0:FXN, :], -4096.0, 4096.0, op0=ALU.mult, op1=ALU.add)
        v.drain()
        v.tensor_tensor(ftmp[0:FXN, :], fidx[0:FXN, :], ftmp[0:FXN, :], op=ALU.add)
        v.drain()
        fxV.inc(v.tensor_copy(fidxs[0:FXN, :], ftmp[0:FXN, :]))
        v.wait_ge(d_fx.h, 32)
        fxV.inc(v.tensor_tensor(grow1[:], trashV[0:1, 0:D], trashS[0:1, 0:D], op=ALU.mult))

    @blk.tensor
    def _(t):
        t.wait_ge(fxV.h, 1)
        fxT.inc(t.matmul(psF[:, 0:1], ones_b[:], fcntb[:], start=True, stop=True))
        for it in range(NITER):
            t.wait_ge(fxV.h, 3 + 2 * it)
            fxT.inc(t.matmul(psF[:, 0:1], ones_b[:], fcntb[:], start=True, stop=True))
        t.wait_ge(fxV.h, 3 + 2 * NITER)
        fxT.inc(t.matmul(psF[:, 0:1], ones_b[:], fcntb[:], start=True, stop=True))
        fxT.inc(t.matmul(psF[:, 1:2], ltri[:], fcntb[:], start=True, stop=True))
        t.wait_ge(fxV.h, 4 + 2 * NITER)
        for sig in range(NT):
            t.matmul(psG[0][0:FXN, 0:1], p8[:, FXN * sig:FXN * (sig + 1)],
                     iota_pb[:], start=(sig == 0), stop=(sig == NT - 1))
            ins = t.matmul(psG[1][0:FXN, 0:1], p8[:, FXN * sig:FXN * (sig + 1)],
                           chunkv[:, sig:sig + 1], start=(sig == 0), stop=(sig == NT - 1))
        fxT.inc(ins)

    @blk.sync
    def _(sp):
        sp.wait_ge(p2V.h, P2V_END)
        d_fx.inc(sp.dma_start(trashV[0:1, 0:D], wlr_h[:]), 16)
        d_fx.inc(sp.dma_start(trashS[0:1, 0:D], wrr_h[:]), 16)
        sp.wait_ge(fxV.h, 6 + 2 * NITER)
        d_fx.inc(sp.dma_start(grow_h[:], grow1[:]), 16)
        sp.wait_ge(d_fx.h, 48)
        for j in range(FXN):
            d_fx.inc(sp.dma_start(grow8[j:j + 1, :], grow_h[:]), 16)
        sp.wait_ge(d_xlo.h, 16 * NT)
        for k in range(NDC):
            if k >= 2:
                sp.wait_ge(fxT.h, NITER + 20 + (k - 2) + 1)
            d_fx.inc(sp.dma_start(xtl_ring[:, (k % 2) * S:(k % 2 + 1) * S],
                                  xntlo_h[128 * k:128 * (k + 1), :]), 16)
        for tb in range(NT):
            if tb >= 2:
                sp.wait_ge(fxT.h, NITER + 28 + (tb - 2) + 1)
            d_fx.inc(sp.dma_start(xnl_ring[:, (tb % 2) * D:(tb % 2 + 1) * D],
                                  xnlo_tm_h[128 * tb:128 * (tb + 1), :]), 16)

    D_FX_G8 = 48 + 16 * FXN
    D_FX_XTL = D_FX_G8 + 16 * NDC
    D_FX_XNL = D_FX_XTL

    GFX = d_g.n

    @blk.gpsimd
    def _(gp):
        gp.wait_ge(fxV.h, 5 + 2 * NITER)
        gp.wait_ge(d_xno.h, 32 * NT)
        d_g.inc(gp.indirect_dma_start(xrh[:], None, xn_tm_h[:], IOff(ap=fidxg[0:FXN, 0:1], axis=0)), 16)
        d_g.inc(gp.indirect_dma_start(xrl[:], None, xnlo_tm_h[:], IOff(ap=fidxg[0:FXN, 0:1], axis=0)), 16)
        d_g.inc(gp.indirect_dma_start(x0r[:], None, x0_h[:], IOff(ap=fidxg[0:FXN, 0:1], axis=0)), 16)
        d_g.inc(gp.dma_start(echo_s[:], iotapf_h[:]), 16)

    @blk.vector
    def _(v):
        v.wait_ge(d_g.h, GFX + 64)
        v.wait_ge(d_fx.h, D_FX_G8)
        v.tensor_tensor(gx[:], xrh[:], xrl[:], op=ALU.add)
        v.drain()
        v.tensor_tensor(gx[:], gx[:], grow8[:], op=ALU.mult)
        v.drain()
        v.tensor_copy(gxh[:], gx[:])
        v.drain()
        fxV.inc(v.tensor_tensor(gxl[:], gx[:], gxh[:], op=ALU.subtract))
        v.drain()
        FB = 8 + 2 * NITER
        for vv in range(2):
            for k in range(NDC):
                j = vv * NDC + k
                v.wait_ge(fxT.h, NITER + 4 + j + 1)
                fxV.inc(v.tensor_copy(lhb[:, HB * vv + FXN * k:HB * vv + FXN * (k + 1)],
                                      psH[j % 2][0:128, 0:FXN]))
        W = FXN * NT
        for k in range(NDC):
            v.wait_ge(fxS.h, k + 1)
            if k == 0:
                fxV.inc(v.tensor_copy(sacc[:], sstg[:, 0:W]))
            else:
                fxV.inc(v.tensor_tensor(sacc[:], sacc[:], sstg[:, (k % 2) * W:(k % 2) * W + W],
                                        op=ALU.add))
            v.drain()
        pac = sstg[:, W:2 * W]
        ptm = sstg[:, 0:W]
        v.tensor_scalar(pac, sacc[:], 1.0 / 16.0, 1.0, op0=ALU.mult, op1=ALU.add)
        v.drain()
        for n in range(15, 0, -1):
            v.tensor_scalar(ptm, sacc[:], 1.0 / n, None, op0=ALU.mult)
            v.drain()
            v.tensor_tensor(pac, pac, ptm, op=ALU.mult)
            v.drain()
            v.tensor_scalar(pac, pac, 1.0, None, op0=ALU.add)
            v.drain()
        v.tensor_copy(ehT[:], pac)
        v.drain()
        fxV.inc(v.tensor_tensor(elT[:], pac, ehT[:], op=ALU.subtract))
        v.drain()
        v.wait_ge(fxT.h, NITER + 60)
        v.tensor_copy(fst[:, 0:1], psF[0:FXN, 4:5])
        v.drain()
        v.wait_ge(fxS.h, 9)
        v.tensor_tensor(fst[:, 2:3], fst[:, 0:1], fst[:, 1:2], op=ALU.mult)
        v.drain()
        v.tensor_scalar(fst[:, 2:3], fst[:, 2:3], -1.0, 2.0, op0=ALU.mult, op1=ALU.add)
        v.drain()
        v.tensor_tensor(fst[:, 1:2], fst[:, 1:2], fst[:, 2:3], op=ALU.mult)
        v.drain()
        v.scalar_tensor_tensor(x1r[:, 0:512], psG[0][0:FXN, 0:512], fst[0:FXN, 1:2],
                               x0r[:, 0:512], op0=ALU.mult, op1=ALU.add)
        v.drain()
        v.scalar_tensor_tensor(x1r[:, 512:1024], psG[1][0:FXN, 0:512], fst[0:FXN, 1:2],
                               x0r[:, 512:1024], op0=ALU.mult, op1=ALU.add)
        v.drain()
        v.reduce_sum(fst[:, 3:4], x1r[:], axis=AXX)
        v.drain()
        fxV.inc(v.tensor_scalar(fst[:, 4:5], fst[:, 3:4], 1.0 / D, None, op0=ALU.mult))
        v.wait_ge(fxS.h, 10)
        v.tensor_tensor(fst[:, 6:7], fst[:, 4:5], fst[:, 4:5], op=ALU.mult)
        v.drain()
        v.scalar_tensor_tensor(fst[:, 7:8], fst[:, 5:6], 1.0 / D, fst[:, 6:7],
                               op0=ALU.mult, op1=ALU.subtract)
        v.drain()
        fxV.inc(v.tensor_scalar(fst[:, 7:8], fst[:, 7:8], 1e-5, None, op0=ALU.add))
        v.wait_ge(fxS.h, 11)
        v.tensor_tensor(fst[:, 9:10], fst[:, 8:9], fst[:, 8:9], op=ALU.mult)
        v.drain()
        v.tensor_tensor(fst[:, 9:10], fst[:, 9:10], fst[:, 7:8], op=ALU.mult)
        v.drain()
        v.tensor_scalar(fst[:, 9:10], fst[:, 9:10], -0.5, 1.5, op0=ALU.mult, op1=ALU.add)
        v.drain()
        v.tensor_tensor(fst[:, 8:9], fst[:, 8:9], fst[:, 9:10], op=ALU.mult)
        v.drain()
        v.tensor_scalar(lnr8[:], x1r[:], fst[0:FXN, 4:5], fst[0:FXN, 8:9],
                        op0=ALU.subtract, op1=ALU.mult)
        v.drain()
        v.tensor_tensor(lnr8[:], lnr8[:], rwb[0:FXN, :], op=ALU.mult)
        v.drain()
        v.reduce_sum(w8[:], lnr8[:], axis=AXX)
        v.drain()
        fxV.inc(v.tensor_tensor(w8[:], w8[:], rb[0:FXN, :], op=ALU.add))

    @blk.scalar
    def _(s):
        W_ = FXN * NT
        for k in range(NDC):
            s.wait_ge(fxT.h, NITER + 20 + k + 1)
            if k >= 2:
                s.wait_ge(fxV.h, 8 + 2 * NITER + 16 + (k - 2))
            fxS.inc(s.activation(sstg[:, (k % 2) * W_:(k % 2) * W_ + W_],
                                 psG[k % 2][:, 0:W_], AF.Copy))
        s.wait_ge(fxT.h, NITER + 60)
        s.activation(fst[:, 1:2], psF[0:FXN, 4:5], AF.Ln)
        s.drain()
        fxS.inc(s.activation(fst[:, 1:2], fst[:, 1:2], AF.Exp, scale=-1.0))
        s.wait_ge(fxV.h, 8 + 2 * NITER + 25)
        fxS.inc(s.activation(trashS[0:FXN, :], x1r[:], AF.Square, accum_out=fst[:, 5:6]))
        s.drain()
        s.wait_ge(fxV.h, 8 + 2 * NITER + 26)
        s.activation(fst[:, 8:9], fst[:, 7:8], AF.Ln)
        s.drain()
        fxS.inc(s.activation(fst[:, 8:9], fst[:, 8:9], AF.Exp, scale=-0.5))

    @blk.tensor
    def _(t):
        t.wait_ge(fxV.h, 7 + 2 * NITER)
        for vv, srcb in enumerate([gxh, gxl]):
            for k in range(NDC):
                j = vv * NDC + k
                if j >= 2:
                    t.wait_ge(fxV.h, 8 + 2 * NITER + (j - 2))
                fxT.inc(t.transpose(psH[j % 2][:, 0:FXN], srcb[:, 128 * k:128 * (k + 1)],
                                    ident[0:FXN, 0:FXN]))
        t.wait_ge(fxV.h, 8 + 2 * NITER + 15)
        for k in range(NDC):
            t.wait_ge(d_fx.h, D_FX_G8 + 16 * (k + 1))
            if k >= 2:
                t.wait_ge(fxS.h, k - 1)
            for sb in range(NT):
                t.matmul(psG[k % 2][:, FXN * sb:FXN * (sb + 1)],
                         xnT[k][:, 128 * sb:128 * (sb + 1)],
                         lhb[:, FXN * k:FXN * (k + 1)], start=True, stop=False)
                t.matmul(psG[k % 2][:, FXN * sb:FXN * (sb + 1)],
                         xtl_ring[:, (k % 2) * S + 128 * sb:(k % 2) * S + 128 * (sb + 1)],
                         lhb[:, FXN * k:FXN * (k + 1)], start=False, stop=False)
                ins = t.matmul(psG[k % 2][:, FXN * sb:FXN * (sb + 1)],
                               xnT[k][:, 128 * sb:128 * (sb + 1)],
                               lhb[:, HB + FXN * k:HB + FXN * (k + 1)], start=False, stop=True)
            fxT.inc(ins)
        t.wait_ge(fxV.h, 8 + 2 * NITER + 24)
        t.wait_ge(fxS.h, 8)
        for tb in range(NT):
            t.wait_ge(d_fx.h, D_FX_XTL + 16 * (tb + 1))
            for hf in range(2):
                t.matmul(psG[hf][0:FXN, :], ehT[:, FXN * tb:FXN * (tb + 1)],
                         xn[:, tb * D + 512 * hf:tb * D + 512 * (hf + 1)],
                         start=(tb == 0), stop=False)
                t.matmul(psG[hf][0:FXN, :], ehT[:, FXN * tb:FXN * (tb + 1)],
                         xnl_ring[:, (tb % 2) * D + 512 * hf:(tb % 2) * D + 512 * (hf + 1)],
                         start=False, stop=False)
                t.matmul(psG[hf][0:FXN, :], elT[:, FXN * tb:FXN * (tb + 1)],
                         xn[:, tb * D + 512 * hf:tb * D + 512 * (hf + 1)],
                         start=False, stop=(tb == NT - 1))
            t.matmul(psF[0:FXN, 4:5], ehT[:, FXN * tb:FXN * (tb + 1)], ones_b[:, 0:1],
                     start=(tb == 0), stop=False)
            ins = t.matmul(psF[0:FXN, 4:5], elT[:, FXN * tb:FXN * (tb + 1)], ones_b[:, 0:1],
                           start=False, stop=(tb == NT - 1))
            fxT.inc(ins)

    @blk.gpsimd
    def _(gp):
        gp.wait_ge(fxV.h, 8 + 2 * NITER + 27)
        d_g.inc(gp.indirect_dma_start(ws_h[:], IOff(ap=fidxs[0:FXN, 0:1], axis=0),
                                      w8[:], None, bounds_check=S - 1, oob_is_err=False), 16)
        d_g.inc(gp.dma_start(echo_s[:], iotapf_h[:]), 16)

    @blk.sync
    def _(sp):
        sp.wait_ge(d_g.h, GFX + 96)
        d_ws.inc(sp.dma_start(wvec[:], bass.AP(ws_h, 0, [[1, 128], [128, NT]]),
                              allow_slow_non_contiguous=True), 16)

    fx.close()


def _emit_ffn(nc, blk, L):
    d_g = L["d_g"]; d_x1w = L["d_x1w"]; d_ws = L["d_ws"]; d_ob = L["d_ob"]
    fT = L["fT"]; fV = L["fV"]; fS = L["fS"]
    aT = L["aT"]; aV = L["aV"]; aS = L["aS"]
    p3V = L["p3V"]
    ident = L["ident"]; iota_pf = L["iota_pf"]
    ln1b = L["ln1b"]; ln2b = L["ln2b"]
    pchunk = L["pchunk"]; idx_i = L["idx_i"]; idxs_i = L["idxs_i"]; cnt_all = L["cnt_all"]
    stat = L["stat"]; trashS = L["trashS"]; trashV = L["trashV"]
    x1s_h = L["x1s_h"]; ws_h = L["ws_h"]; pos_h = L["pos_h"]; mask_h = L["mask_h"]
    ln1b_h = L["ln1b_h"]; ln2b_h = L["ln2b_h"]; d_fx = L["d_fx"]
    echo_s = L["echo_s"]; echo_s2 = echo_s; iotapf_h = L["iotapf_h"]
    cost_h = L["cost_h"]; sints_h = L["sints_h"]; out_h = L["out_h"]
    wq_h = L["wq_h"]; wk_h = L["wk_h"]; wv_h = L["wv_h"]; wo_h = L["wo_h"]
    w1_h = L["w1_h"]; w2_h = L["w2_h"]
    x1sel = L["x1sel"]; xsel = L["xsel"]; x1f = L["x1f"]
    wselv = L["wselv"]; poss = L["poss"]; mselT = L["mselT"]
    psT = L["psT"]; psM = L["psM"]; psQ = L["psQ"]
    gsel2 = L["gsel2"]; fstk = L["fstk"]; stk_sel = L["stk_sel"]
    P3V_DONE = p3V.n

    ga = ExitStack()
    mrows = ga.enter_context(nc.sbuf_tensor("mrows", [128, NB * S], BF16))
    mt = ga.enter_context(nc.sbuf_tensor("mt", [128, NT * 512], BF16))

    GA0 = d_g.n
    LN_GATE = d_fx.n + 32

    @blk.sync
    def _(sp):
        sp.wait_ge(p3V.h, P3V_DONE)
        d_fx.inc(sp.dma_start(ln1b[:], ln1b_h[:]), 16)
        d_fx.inc(sp.dma_start(ln2b[:], ln2b_h[:]), 16)

    @blk.gpsimd
    def _(gp):
        gp.wait_ge(p3V.h, P3V_DONE)
        gp.wait_ge(d_g.h, GA0)
        gp.wait_ge(d_x1w.h, 16 * NT)
        gp.wait_ge(d_ob.h, 16 * NT)
        gp.wait_ge(d_ws.h, 32)
        for B in range(NB):
            ib = idx_i[:, B:B + 1]
            d_g.inc(gp.indirect_dma_start(x1sel[:, B * D:(B + 1) * D], None,
                                          x1s_h[:], IOff(ap=ib, axis=0)), 16)
            d_g.inc(gp.indirect_dma_start(wselv[:, B:B + 1], None,
                                          ws_h[:], IOff(ap=ib, axis=0)), 16)
            d_g.inc(gp.indirect_dma_start(poss[:, B:B + 1], None,
                                          pos_h[:], IOff(ap=ib, axis=0)), 16)
            d_g.inc(gp.indirect_dma_start(mrows[:, B * S:(B + 1) * S], None,
                                          mask_h[:], IOff(ap=ib, axis=0)), 16)
            d_g.inc(gp.dma_start(echo_s[:], iotapf_h[:]), 16)

    @blk.tensor
    def _(t):
        for B in range(NB):
            t.wait_ge(d_g.h, GA0 + 16 * (5 * B + 5))
            for c in range(NT):
                j = B * NT + c
                if j >= 2:
                    t.wait_ge(fV.h, j - 1)
                fT.inc(t.transpose(psT[j % 2][:, 0:128], mrows[:, B * S + 128 * c:B * S + 128 * (c + 1)], ident[:]))
        t.wait_ge(fV.h, NB * NT)
        for kb in range(NB):
            if kb >= 2:
                t.wait_ge(fV.h, NB * NT + kb - 1)
            for c in range(NT):
                ins = t.matmul(psM[kb % 2][:],
                               pchunk[:, 512 * c + 128 * kb:512 * c + 128 * (kb + 1)],
                               mt[:, 512 * c:512 * (c + 1)],
                               start=(c == 0), stop=(c == NT - 1))
            fT.inc(ins)

    @blk.vector
    def _(v):
        v.memset(stat[:, 15:16], 1e-6)
        v.drain()
        v.memset(stat[:, 13:14], 1e-30)
        v.drain()
        for B in range(NB):
            for c in range(NT):
                j = B * NT + c
                v.wait_ge(fT.h, j + 1)
                fV.inc(v.tensor_copy(mt[:, 512 * c + 128 * B:512 * c + 128 * (B + 1)], psT[j % 2][:, 0:128]))
        for kb in range(NB):
            v.wait_ge(fT.h, NB * NT + kb + 1)
            v.tensor_copy(mselT[:, 512 * kb:512 * (kb + 1)], psM[kb % 2][:])
            v.tensor_tensor(stat[:, 14:15], iota_pf[:], cnt_all[:], op=ALU.subtract)
            v.drain()
            v.tensor_scalar(stat[:, 14:15], stat[:, 14:15], float(-128 * kb), None, op0=ALU.is_ge)
            v.drain()
            v.tensor_scalar(stat[:, 14:15], stat[:, 14:15], -1e9, None, op0=ALU.mult)
            v.drain()
            fV.inc(v.tensor_scalar(mselT[:, 512 * kb:512 * (kb + 1)],
                                   mselT[:, 512 * kb:512 * (kb + 1)],
                                   stat[:, 14:15], None, op0=ALU.add))
    FT_A = fT.n
    FV_A = fV.n

    ga.close()
    gsel2.close()
    gb1 = ExitStack()
    xhT = gb1.enter_context(nc.sbuf_tensor("xhT", [128, NDC * 512], BF16))
    qt = gb1.enter_context(nc.sbuf_tensor("qt", [128, NDC * 512], BF16))
    kt = gb1.enter_context(nc.sbuf_tensor("kt", [128, NDC * 512], BF16))
    vaug = gb1.enter_context(nc.sbuf_tensor("vaug", [128, NB * H * 65], BF16))
    xhbf = gb1.enter_context(nc.sbuf_tensor("xhbf", [128, D], BF16))
    xhf = gb1.enter_context(nc.sbuf_tensor("xhf", [128, D], F32))
    qf = gb1.enter_context(nc.sbuf_tensor("qf", [128, D], F32))
    kf = gb1.enter_context(nc.sbuf_tensor("kf", [128, D], F32))
    rt1 = gb1.enter_context(nc.sbuf_tensor("rt1", [128, D], F32))
    cosr = gb1.enter_context(nc.sbuf_tensor("cosr", [128, D], F32))
    sinr = gb1.enter_context(nc.sbuf_tensor("sinr", [128, D], F32))
    qrb = gb1.enter_context(nc.sbuf_tensor("qrb", [128, D], BF16))
    krb = gb1.enter_context(nc.sbuf_tensor("krb", [128, D], BF16))
    vbb = gb1.enter_context(nc.sbuf_tensor("vbb", [128, D], BF16))
    attn = gb1.enter_context(nc.sbuf_tensor("attn", [128, NB * D], BF16))
    gb2 = ExitStack()
    wqb = gb2.enter_context(nc.sbuf_tensor("wqb", [128, NDC * D], BF16))
    wkb = gb2.enter_context(nc.sbuf_tensor("wkb", [128, NDC * D], BF16))
    wvb = gb2.enter_context(nc.sbuf_tensor("wvb", [128, NDC * D], BF16))

    GB_W = d_g.n + 16 * 3

    @blk.gpsimd
    def _(gp):
        gp.wait_ge(fT.h, FT_A)
        gp.wait_ge(fV.h, FV_A)
        for dst, src in [(wqb, wq_h), (wkb, wk_h), (wvb, wv_h)]:
            d_g.inc(gp.dma_start(dst[:], bass.AP(src, 0, [[D, 128], [128 * D, NDC], [1, D]])), 16)
        for B in range(NB):
            if B >= 1:
                gp.wait_ge(fV.h, FV_A + 30 * (B - 1) + 13)
            d_g.inc(gp.indirect_dma_start(cosr[:], None,
                                          cost_h[:], IOff(ap=poss[:, B:B + 1], axis=0)), 16)
            d_g.inc(gp.indirect_dma_start(sinr[:], None,
                                          sints_h[:], IOff(ap=poss[:, B:B + 1], axis=0)), 16)
            d_g.inc(gp.dma_start(echo_s2[:], iotapf_h[:]), 16)

    for B in range(NB):
        FT_B = fT.n
        FV_B = fV.n
        FS_B = fS.n

        @blk.vector
        def _(v, B=B, FT_B=FT_B, FV_B=FV_B, FS_B=FS_B):
            xr = x1sel[:, B * D:(B + 1) * D]
            v.wait_ge(d_g.h, GA0 + 16 * (5 * B + 5))
            v.reduce_sum(stat[:, 0:1], xr, axis=AXX)
            v.drain()
            v.tensor_scalar(stat[:, 1:2], stat[:, 0:1], 1.0 / D, None, op0=ALU.mult)
            v.drain()
            v.wait_ge(fS.h, FS_B + 1)
            v.tensor_tensor(stat[:, 3:4], stat[:, 1:2], stat[:, 1:2], op=ALU.mult)
            v.drain()
            v.scalar_tensor_tensor(stat[:, 4:5], stat[:, 2:3], 1.0 / D,
                                   stat[:, 3:4], op0=ALU.mult, op1=ALU.subtract)
            v.drain()
            fV.inc(v.tensor_scalar(stat[:, 4:5], stat[:, 4:5], 1e-5, None, op0=ALU.add))
            v.wait_ge(fS.h, FS_B + 2)
            fV.inc(v.tensor_scalar(xsel[:, B * D:(B + 1) * D], xr,
                                   stat[:, 1:2], stat[:, 5:6],
                                   op0=ALU.subtract, op1=ALU.mult))
            v.wait_ge(fS.h, FS_B + 4)
            if B == 0:
                v.wait_ge(d_fx.h, LN_GATE)
            if B >= 1:
                v.wait_ge(fT.h, FT_B - 27 + 8)
            v.tensor_scalar(xhf[:], xsel[:, B * D:(B + 1) * D], stat[:, 7:8], None, op0=ALU.mult)
            v.drain()
            fV.inc(v.tensor_tensor(xhbf[:], xhf[:], ln1b[:], op=ALU.mult))

        @blk.scalar
        def _(s, B=B, FT_B=FT_B, FV_B=FV_B, FS_B=FS_B):
            s.wait_ge(d_g.h, GA0 + 16 * (5 * B + 5))
            fS.inc(s.activation(trashS[:], x1sel[:, B * D:(B + 1) * D], AF.Square,
                                accum_out=stat[:, 2:3]))
            s.wait_ge(fV.h, FV_B + 1)
            s.activation(stat[:, 5:6], stat[:, 4:5], AF.Ln)
            s.drain()
            fS.inc(s.activation(stat[:, 5:6], stat[:, 5:6], AF.Exp, scale=-0.5))
            s.wait_ge(fV.h, FV_B + 2)
            fS.inc(s.activation(trashS[:], xsel[:, B * D:(B + 1) * D], AF.Square,
                                accum_out=stat[:, 6:7]))
            s.drain()
            s.activation(stat[:, 7:8], stat[:, 6:7], AF.Ln, bias=stat[:, 15:16], scale=1.0 / D)
            s.drain()
            fS.inc(s.activation(stat[:, 7:8], stat[:, 7:8], AF.Exp, scale=-0.5))
            s.wait_ge(fT.h, FT_B + 9)
            s.activation(qf[:, 0:512], psQ[0][:], AF.Copy, scale=0.125)
            fS.inc(s.activation(qf[:, 512:1024], psQ[1][:], AF.Copy, scale=0.125))
            s.wait_ge(fT.h, FT_B + 10)
            s.activation(kf[:, 0:512], psQ[0][:], AF.Copy)
            fS.inc(s.activation(kf[:, 512:1024], psQ[1][:], AF.Copy))
            s.wait_ge(fT.h, FT_B + 11)
            s.activation(vbb[:, 0:512], psQ[0][:], AF.Copy)
            fS.inc(s.activation(vbb[:, 512:1024], psQ[1][:], AF.Copy))

        @blk.tensor
        def _(t, B=B, FT_B=FT_B, FV_B=FV_B, FS_B=FS_B):
            t.wait_ge(fV.h, FV_B + 3)
            for k in range(NDC):
                if k >= 2:
                    t.wait_ge(fV.h, FV_B + k + 2)
                fT.inc(t.transpose(psT[k % 2][:, 0:128], xhbf[:, 128 * k:128 * (k + 1)], ident[:]))
            t.wait_ge(fV.h, FV_B + 11)
            t.wait_ge(d_g.h, GB_W)
            for wsb in (wqb, wkb, wvb):
                if wsb is wkb:
                    t.wait_ge(fS.h, FS_B + 5)
                if wsb is wvb:
                    t.wait_ge(fS.h, FS_B + 6)
                for hf in range(2):
                    for k in range(NDC):
                        ins = t.matmul(psQ[hf][:],
                                       xhT[:, 512 * k + 128 * B:512 * k + 128 * (B + 1)],
                                       wsb[:, k * D + 512 * hf:k * D + 512 * (hf + 1)],
                                       start=(k == 0), stop=(k == NDC - 1))
                fT.inc(ins)
            t.wait_ge(fV.h, FV_B + 13)
            for j, src_t in enumerate([qrb, krb]):
                for k in range(NDC):
                    jj = j * NDC + k + 1
                    if jj >= 3:
                        t.wait_ge(fV.h, FV_B + 12 + jj)
                    fT.inc(t.transpose(psT[(jj - 1) % 2][:, 0:128], src_t[:, 128 * k:128 * (k + 1)], ident[:]))

        @blk.vector
        def _(v, B=B, FT_B=FT_B, FV_B=FV_B, FS_B=FS_B):
            for k in range(NDC):
                v.wait_ge(fT.h, FT_B + k + 1)
                fV.inc(v.tensor_copy(xhT[:, 512 * k + 128 * B:512 * k + 128 * (B + 1)],
                                     psT[k % 2][:, 0:128]))
            for nm, (fsrc, dstb) in enumerate([(qf, qrb), (kf, krb)]):
                v.wait_ge(fS.h, FS_B + 5 + nm)
                if nm == 0:
                    v.wait_ge(d_g.h, GB_W + 48 * (B + 1))
                v.tensor_tensor(rt1[:], fsrc[:], cosr[:], op=ALU.mult)
                v.tensor_tensor(bass.AP(trashV, 0, [[D, 128], [64, 16], [1, 32]]),
                                bass.AP(fsrc, 32, [[D, 128], [64, 16], [1, 32]]),
                                bass.AP(sinr, 0, [[D, 128], [64, 16], [1, 32]]),
                                op=ALU.mult)
                v.tensor_tensor(bass.AP(trashV, 32, [[D, 128], [64, 16], [1, 32]]),
                                bass.AP(fsrc, 0, [[D, 128], [64, 16], [1, 32]]),
                                bass.AP(sinr, 32, [[D, 128], [64, 16], [1, 32]]),
                                op=ALU.mult)
                v.drain()
                fV.inc(v.tensor_tensor(dstb[:], rt1[:], trashV[:], op=ALU.add))
            v.wait_ge(fS.h, FS_B + 7)
            v.memset(vaug[:, B * H * 65:(B + 1) * H * 65], 1.0)
            v.drain()
            for h in range(H):
                ins = v.tensor_copy(vaug[:, B * H * 65 + 65 * h:B * H * 65 + 65 * h + 64],
                                    vbb[:, 64 * h:64 * (h + 1)])
            fV.inc(ins)
            for j, dst in enumerate([qt, kt]):
                for k in range(NDC):
                    jj = j * NDC + k + 1
                    v.wait_ge(fT.h, FT_B + 11 + jj)
                    fV.inc(v.tensor_copy(dst[:, 512 * k + 128 * B:512 * k + 128 * (B + 1)],
                                         psT[(jj - 1) % 2][:, 0:128]))

    FT_BQ = fT.n
    FV_BQ = fV.n
    FS_BQ = fS.n

    gb2.close()
    gc = ExitStack()
    etb = gc.enter_context(nc.sbuf_tensor("etb", [128, 2 * 4 * 512], BF16))
    scf = gc.enter_context(nc.sbuf_tensor("scf", [128, 2 * 512], F32))
    invzh = gc.enter_context(nc.sbuf_tensor("invzh", [128, 2], F32))
    psA = [gc.enter_context(nc.psum_tensor(f"psA{i}", [128, 512], F32)) for i in range(2)]

    @blk.tensor
    def _(t):
        t.wait_ge(fV.h, FV_BQ)
        for h in range(H):
            hp = 64 * (h % 2)
            cbase = 512 * (h // 2)
            for kb in range(4):
                sidx = 4 * h + kb
                if sidx >= 2:
                    t.wait_ge(aV.h, sidx - 1)
                aT.inc(t.matmul(psM[sidx % 2][:],
                                kt[hp:hp + 64, cbase + 128 * kb:cbase + 128 * (kb + 1)],
                                qt[hp:hp + 64, cbase:cbase + 512],
                                start=True, stop=True))
            for qb in range(4):
                a = 4 * h + qb
                w = 8 * h + 4
                if qb >= 2:
                    w = max(w, 8 * h + qb + 3)
                elif h >= 1:
                    w = max(w, 8 * h + qb - 1)
                t.wait_ge(aS.h, w)
                for kb in range(4):
                    ins = t.matmul(psA[a % 2][:, 0:65],
                                   etb[:, (h % 2) * 2048 + 512 * kb + 128 * qb:
                                        (h % 2) * 2048 + 512 * kb + 128 * (qb + 1)],
                                   vaug[:, kb * H * 65 + 65 * h:kb * H * 65 + 65 * (h + 1)],
                                   start=(kb == 0), stop=(kb == 3))
                aT.inc(ins)

    @blk.vector
    def _(v):
        for h in range(H):
            for kb in range(4):
                sidx = 4 * h + kb
                v.wait_ge(aT.h, 8 * h + kb + 1)
                if sidx >= 2:
                    if kb >= 2:
                        v.wait_ge(aS.h, 8 * h + kb - 1)
                    elif h >= 1:
                        v.wait_ge(aS.h, 8 * (h - 1) + kb + 3)
                aV.inc(v.tensor_tensor(scf[:, (sidx % 2) * 512:(sidx % 2 + 1) * 512],
                                       psM[sidx % 2][:],
                                       mselT[:, 512 * kb:512 * (kb + 1)], op=ALU.add))

    @blk.scalar
    def _(s):
        for h in range(H):
            for kb in range(4):
                sidx = 4 * h + kb
                s.wait_ge(aV.h, sidx + 1)
                if h >= 2 and kb == 0:
                    s.wait_ge(aT.h, 8 * (h - 1))
                aS.inc(s.activation(etb[:, (h % 2) * 2048 + 512 * kb:(h % 2) * 2048 + 512 * (kb + 1)],
                                    scf[:, (sidx % 2) * 512:(sidx % 2 + 1) * 512], AF.Exp))
            for qb in range(4):
                a = 4 * h + qb
                s.wait_ge(aT.h, 8 * h + 4 + qb + 1)
                s.activation(invzh[:, a % 2:a % 2 + 1], psA[a % 2][:, 64:65], AF.Ln,
                             bias=stat[:, 13:14])
                s.drain()
                s.activation(invzh[:, a % 2:a % 2 + 1], invzh[:, a % 2:a % 2 + 1], AF.Exp, scale=-1.0)
                s.drain()
                aS.inc(s.activation(attn[:, qb * D + 64 * h:qb * D + 64 * (h + 1)],
                                    psA[a % 2][:, 0:64], AF.Copy,
                                    scale=invzh[:, a % 2:a % 2 + 1]))
    AT_C = aT.n
    AS_C = aS.n

    gc.close()
    gd1 = ExitStack()
    wob = gd1.enter_context(nc.sbuf_tensor("wob", [128, NDC * D], BF16))
    atT = gd1.enter_context(nc.sbuf_tensor("atT", [128, NDC * 512], BF16))

    FT_D = fT.n
    FV_D = fV.n
    FS_D = fS.n

    GD_WO = d_g.n + 16

    @blk.gpsimd
    def _(gp):
        gp.wait_ge(fT.h, FT_BQ)
        gp.wait_ge(aT.h, AT_C)
        gp.wait_ge(aS.h, AS_C)
        d_g.inc(gp.dma_start(wob[:], bass.AP(wo_h, 0, [[D, 128], [128 * D, NDC], [1, D]])), 16)

    @blk.tensor
    def _(t):
        t.wait_ge(aS.h, AS_C)
        for B in range(NB):
            for k in range(NDC):
                j = B * NDC + k
                if j >= 2:
                    t.wait_ge(fV.h, FV_D + j - 1)
                fT.inc(t.transpose(psT[j % 2][:, 0:128], attn[:, B * D + 128 * k:B * D + 128 * (k + 1)],
                                   ident[:]))
        t.wait_ge(d_g.h, GD_WO)
        for B in range(NB):
            t.wait_ge(fV.h, FV_D + 32)
            if B >= 1:
                t.wait_ge(fV.h, FV_D + 32 + B)
            for hf in range(2):
                for k in range(NDC):
                    ins = t.matmul(psQ[hf][:],
                                   atT[:, 512 * k + 128 * B:512 * k + 128 * (B + 1)],
                                   wob[:, k * D + 512 * hf:k * D + 512 * (hf + 1)],
                                   start=(k == 0), stop=(k == NDC - 1))
            fT.inc(ins)

    @blk.vector
    def _(v):
        for B in range(NB):
            for k in range(NDC):
                j = B * NDC + k
                v.wait_ge(fT.h, FT_D + j + 1)
                fV.inc(v.tensor_copy(atT[:, 512 * k + 128 * B:512 * k + 128 * (B + 1)],
                                     psT[j % 2][:, 0:128]))
        for B in range(NB):
            v.wait_ge(fT.h, FT_D + 32 + B + 1)
            for hf in range(2):
                ins = v.tensor_tensor(x1f[:, B * D + 512 * hf:B * D + 512 * (hf + 1)],
                                      psQ[hf][:],
                                      xsel[:, B * D + 512 * hf:B * D + 512 * (hf + 1)], op=ALU.add)
            fV.inc(ins)

    @blk.scalar
    def _(s):
        for B in range(NB):
            s.wait_ge(fV.h, FV_D + 33 + B)
            if B >= 1:
                s.wait_ge(fV.h, FV_D + 36 + B)
            fS.inc(s.activation(trashS[:], x1f[:, B * D:(B + 1) * D], AF.Square,
                                accum_out=stat[:, 8:9]))
            s.drain()
            s.activation(stat[:, 9:10], stat[:, 8:9], AF.Ln, bias=stat[:, 15:16], scale=1.0 / D)
            s.drain()
            fS.inc(s.activation(stat[:, 9:10], stat[:, 9:10], AF.Exp, scale=-0.5))

    gd1.close()
    gb1.close()
    gd2 = ExitStack()
    x2hT = gd2.enter_context(nc.sbuf_tensor("x2hT", [128, NDC * 512], BF16))
    xh2f = gd2.enter_context(nc.sbuf_tensor("xh2f", [128, D], F32))
    xh2bf = gd2.enter_context(nc.sbuf_tensor("xh2bf", [128, NB * D], BF16))
    orow = gd2.enter_context(nc.sbuf_tensor("orow", [128, 2 * D], F32))
    ge1a = ExitStack()
    guT = ge1a.enter_context(nc.sbuf_tensor("guT", [128, NFC * 512], BF16))
    ge1b = ExitStack()
    w1b = ge1b.enter_context(nc.sbuf_tensor("w1b", [128, NDC * FF], BF16))

    @blk.vector
    def _(v):
        v.wait_ge(fT.h, FT_D + 36)
        for B in range(NB):
            v.wait_ge(fS.h, FS_D + 2 * B + 2)
            v.tensor_scalar(xh2f[:], x1f[:, B * D:(B + 1) * D], stat[:, 9:10], None, op0=ALU.mult)
            v.drain()
            fV.inc(v.tensor_tensor(xh2bf[:, B * D:(B + 1) * D], xh2f[:], ln2b[:], op=ALU.mult))
        for B in range(NB):
            for k in range(NDC):
                j = B * NDC + k
                v.wait_ge(fT.h, FT_D + 36 + j + 1)
                fV.inc(v.tensor_copy(x2hT[:, 512 * k + 128 * B:512 * k + 128 * (B + 1)],
                                     psT[j % 2][:, 0:128]))

    @blk.tensor
    def _(t):
        for B in range(NB):
            t.wait_ge(fV.h, FV_D + 37 + B)
            for k in range(NDC):
                j = B * NDC + k
                if j >= 2:
                    t.wait_ge(fV.h, FV_D + 40 + j - 1)
                fT.inc(t.transpose(psT[j % 2][:, 0:128], xh2bf[:, B * D + 128 * k:B * D + 128 * (k + 1)],
                                   ident[:]))

    GE_W1 = d_g.n + 16

    @blk.gpsimd
    def _(gp):
        gp.wait_ge(fT.h, FT_D + 36)
        gp.wait_ge(aT.h, AT_C)
        d_g.inc(gp.dma_start(w1b[:], bass.AP(w1_h, 0, [[FF, 128], [128 * FF, NDC], [1, FF]])), 16)

    @blk.tensor
    def _(t):
        t.wait_ge(d_g.h, GE_W1)
        t.wait_ge(fV.h, FV_D + 72)
        for fbk in range(NFC):
            if fbk >= 2:
                t.wait_ge(fS.h, FS_D + 8 + fbk - 1)
            for k in range(NDC):
                ins = t.matmul(psM[fbk % 2][:],
                               w1b[:, k * FF + 128 * fbk:k * FF + 128 * (fbk + 1)],
                               x2hT[:, 512 * k:512 * (k + 1)],
                               start=(k == 0), stop=(k == NDC - 1))
            fT.inc(ins)

    @blk.scalar
    def _(s):
        for fbk in range(NFC):
            s.wait_ge(fT.h, FT_D + 68 + fbk + 1)
            fS.inc(s.activation(guT[:, 512 * fbk:512 * (fbk + 1)], psM[fbk % 2][:],
                                AF.Gelu_apprx_tanh))

    ge1b.close()
    ge2 = ExitStack()
    w2b = ge2.enter_context(nc.sbuf_tensor("w2b", [128, NFC * D], BF16))
    GE_W2 = d_g.n + 16

    @blk.gpsimd
    def _(gp):
        gp.wait_ge(fT.h, FT_D + 100)
        d_g.inc(gp.dma_start(w2b[:], bass.AP(w2_h, 0, [[D, 128], [128 * D, NFC], [1, D]])), 16)

    @blk.tensor
    def _(t):
        t.wait_ge(d_g.h, GE_W2)
        t.wait_ge(fS.h, FS_D + 40)
        for B in range(NB):
            if B >= 1:
                t.wait_ge(fV.h, FV_D + 72 + B)
            for hf in range(2):
                for fc in range(NFC):
                    ins = t.matmul(psQ[hf][:],
                                   guT[:, 512 * fc + 128 * B:512 * fc + 128 * (B + 1)],
                                   w2b[:, fc * D + 512 * hf:fc * D + 512 * (hf + 1)],
                                   start=(fc == 0), stop=(fc == NFC - 1))
            fT.inc(ins)

    GD_SC = d_g.n

    @blk.vector
    def _(v):
        for B in range(NB):
            v.wait_ge(fT.h, FT_D + 100 + B + 1)
            if B >= 2:
                v.wait_ge(d_g.h, GD_SC + 16 * (B - 1))
            o = (B % 2) * D
            for hf in range(2):
                v.tensor_tensor(orow[:, o + 512 * hf:o + 512 * (hf + 1)], psQ[hf][:],
                                x1f[:, B * D + 512 * hf:B * D + 512 * (hf + 1)], op=ALU.add)
            v.drain()
            v.tensor_scalar(orow[:, o:o + D], orow[:, o:o + D], wselv[:, B:B + 1], None, op0=ALU.mult)
            v.drain()
            fV.inc(v.tensor_tensor(orow[:, o:o + D], orow[:, o:o + D],
                                   x1sel[:, B * D:(B + 1) * D], op=ALU.add))

    @blk.gpsimd
    def _(gp):
        gp.wait_ge(d_ob.h, 16 * NT)
        for B in range(NB):
            gp.wait_ge(fV.h, FV_D + 72 + B + 1)
            o = (B % 2) * D
            d_g.inc(gp.indirect_dma_start(out_h[:], IOff(ap=idxs_i[:, B:B + 1], axis=0),
                                          orow[:, o:o + D], None,
                                          bounds_check=S - 1, oob_is_err=False), 16)
        d_g.inc(gp.dma_start(echo_s2[:], iotapf_h[:]), 16)

    ge2.close()
    ge1a.close()
    gd2.close()
    fstk.close()
    stk_sel.close()


def _shard_inputs(inputs):
    c = _host_constants()
    x = np.asarray(inputs["L0_attn_x"])
    wl = np.asarray(inputs["L0_attn_w_lhs"]).reshape(D)
    wr = np.asarray(inputs["L0_attn_w_rhs"]).reshape(D)
    mask = np.asarray(inputs["L0_ffn_attention_mask"])
    pos = np.asarray(inputs["L0_ffn_position_ids"])
    rw = np.asarray(inputs["L0_ffn_router_w"]).reshape(D)
    rbv = np.asarray(inputs["L0_ffn_router_b"]).reshape(1)
    ln1 = np.asarray(inputs["L0_ffn_ln1"]).reshape(D)
    ln2 = np.asarray(inputs["L0_ffn_ln2"]).reshape(D)

    def pmj(v):
        return np.ascontiguousarray(v.reshape(NDC, 128).T)

    shared = {
        "wl": pmj(wl), "wr": pmj(wr),
        "wlr": np.ascontiguousarray(wl.reshape(1, D)),
        "wrr": np.ascontiguousarray(wr.reshape(1, D)),
        "rwb": np.ascontiguousarray(np.broadcast_to(rw[None, :], (128, D))),
        "rb": np.full((128, 1), rbv[0], np.float32),
        "ln1b": np.ascontiguousarray(np.broadcast_to(ln1[None, :], (128, D))),
        "ln2b": np.ascontiguousarray(np.broadcast_to(ln2[None, :], (128, D))),
        "wq": np.ascontiguousarray(inputs["L0_ffn_wq"]),
        "wk": np.ascontiguousarray(inputs["L0_ffn_wk"]),
        "wv": np.ascontiguousarray(inputs["L0_ffn_wv"]),
        "wo": np.ascontiguousarray(inputs["L0_ffn_wo"]),
        "w1": np.ascontiguousarray(inputs["L0_ffn_w1"]),
        "w2": np.ascontiguousarray(inputs["L0_ffn_w2"]),
        "IDENT": c["IDENT"], "ONES": c["ONES"], "LTRI": c["LTRI"],
        "IOTA_F": c["IOTA_F"], "IOTA_P_F": c["IOTA_P_F"], "IOTA_P_B": c["IOTA_P_B"],
        "CHUNKV_B": c["CHUNKV_B"], "COS_T": c["COS_T"], "SIN_TS": c["SIN_TS"],
    }
    in_maps = []
    for b in range(8):
        mb = min(b, 3)
        m = dict(shared)
        m["x0"] = np.ascontiguousarray(x[b])
        m["maskb"] = np.ascontiguousarray(mask[mb, 0])
        m["posb"] = np.ascontiguousarray(pos[mb].reshape(S, 1))
        in_maps.append(m)
    return in_maps


_NC_CACHE = {}


def kernel(**inputs):
    if "nc" not in _NC_CACHE:
        _NC_CACHE["nc"] = build_program()
    nc = _NC_CACHE["nc"]
    in_maps = _shard_inputs(inputs)
    res = run_bass_kernel_spmd(nc, in_maps, core_ids=list(range(8)))
    return np.stack([res.results[b]["out"] for b in range(8)], axis=0)
